# version 03
# baseline (speedup 1.0000x reference)
import numpy as np
import ml_dtypes
import concourse.bass as bass
import concourse.mybir as mybir
from concourse.bass_utils import run_bass_kernel_spmd

F32 = mybir.dt.float32
BF16 = mybir.dt.bfloat16
I32 = mybir.dt.int32
I8 = mybir.dt.int8
AF = mybir.ActivationFunctionType
ALU = mybir.AluOpType

NCORE = 8
T = 16384
TL = T // NCORE
NT = TL // 128
NG = NT // 4
D = 1024
DC = D // 128
H = 16
DH = 64
FF = 4096
FC = FF // 128

LN_EPS = 1e-5
L0FFN_EPS = 1e-6
L0ATTN_EPS = 1e-6
RMS_EPS = 1e-6
AQ_EPS = 1e-5
WQ_EPS = 1e-5

PHASES = 6
DBG_STEP = 'full'


def _ceil32(x):
    return (x + 31) & ~31


class Ctr:

    def __init__(self):
        self.v = {}

    def inc(self, k, n=1):
        self.v[k] = self.v.get(k, 0) + n
        return self.v[k]

    def get(self, k):
        return self.v.get(k, 0)


def _build(phases=PHASES, detect_races=False):
    nc = bass.Bass("TRN2", target_bir_lowering=False,
                   detect_race_conditions=detect_races)

    x_in = nc.dram_tensor("x", [TL, D], F32, kind="ExternalInput")
    y_out = nc.dram_tensor("y", [TL, D], F32, kind="ExternalOutput")

    l0w8 = nc.dram_tensor("l0w8", [D, D], I8, kind="ExternalInput")
    l0ws = nc.dram_tensor("l0ws", [D], F32, kind="ExternalInput")
    l0fb = nc.dram_tensor("l0fb", [D], F32, kind="ExternalInput")
    aw = {}
    ab = {}
    for nm in ("q", "k", "v", "o"):
        aw[nm] = nc.dram_tensor(f"l0a_w{nm}", [D, D], F32, kind="ExternalInput")
        ab[nm] = nc.dram_tensor(f"l0a_b{nm}", [D], F32, kind="ExternalInput")
    bw = {}
    for nm in ("q", "k", "v", "o"):
        bw[nm] = nc.dram_tensor(f"l1a_w{nm}", [D, D], F32, kind="ExternalInput")
    w1 = nc.dram_tensor("w1", [D, FF], F32, kind="ExternalInput")
    b1 = nc.dram_tensor("b1", [FF], F32, kind="ExternalInput")
    w2 = nc.dram_tensor("w2", [FF, D], F32, kind="ExternalInput")
    b2 = nc.dram_tensor("b2", [D], F32, kind="ExternalInput")

    spill_x = nc.dram_tensor("spill_x", [128, NT * D], F32)
    tern_sp = {nm: nc.dram_tensor(f"tern_{nm}", [128, DC * D], BF16)
               for nm in ("q", "k", "v", "o")}
    agk_in = nc.dram_tensor("agk_in", [D, TL], BF16)
    agk_out = nc.dram_tensor("agk_out", [NCORE * D, TL], BF16)
    agv_in = nc.dram_tensor("agv_in", [TL, H * 65], BF16)
    agv_out = nc.dram_tensor("agv_out", [T, H * 65], BF16)
    kv_in = nc.dram_tensor("kv_in", [128, 528], F32)
    kv_out = nc.dram_tensor("kv_out", [128, 528], F32)
    norm_dram = nc.dram_tensor("norm_dram", [H, TL], F32)
    dbg_out = nc.dram_tensor("dbg", [128, 3072], F32, kind="ExternalOutput")

    identb = nc.inline_tensor(np.eye(128, dtype=ml_dtypes.bfloat16), name="identb")
    ident32 = nc.inline_tensor(np.eye(128, dtype=np.float32), name="ident32")
    ones128 = nc.inline_tensor(np.ones((128, 128), dtype=np.float32), name="ones128")
    onescol_b = nc.inline_tensor(np.ones((128, 1), dtype=ml_dtypes.bfloat16), name="onescol_b")

    from contextlib import ExitStack
    with ExitStack() as stk:
        E = stk.enter_context
        A1 = E(nc.sbuf_tensor("A1", [128, 65536], mybir.dt.uint8))
        A2 = E(nc.sbuf_tensor("A2", [128, 65536], mybir.dt.uint8))
        A4 = E(nc.sbuf_tensor("A4", [128, 32768], mybir.dt.uint8))
        A5 = E(nc.sbuf_tensor("A5", [128, 32768], mybir.dt.uint8))
        A6 = E(nc.sbuf_tensor("A6", [128, 14464], mybir.dt.uint8))

        def V(arena, off_bytes, dtype, cols):
            ap = arena.ap()[:, off_bytes:off_bytes + cols * mybir.dt.size(dtype)]
            return ap.bitcast(dtype)

        identb_t = E(nc.sbuf_tensor("identb_t", [128, 128], BF16))
        ident32_t = E(nc.sbuf_tensor("ident32_t", [128, 128], F32))
        ones128_t = E(nc.sbuf_tensor("ones128_t", [128, 128], F32))
        onescol_t = E(nc.sbuf_tensor("onescol_t", [128, 1], BF16))

        ln_sc = E(nc.sbuf_tensor("ln_sc", [128, 16], F32))
        swrec = E(nc.sbuf_tensor("swrec", [128, 4], F32))
        swinv = E(nc.sbuf_tensor("swinv", [128, 4], F32))
        bcol = E(nc.sbuf_tensor("bcol", [128, 8 * 4 + 32 + 8 + 1], F32))
        bq_col = bcol[:, 0:8]
        bk_col = bcol[:, 8:16]
        bv_col = bcol[:, 16:24]
        bo_col = bcol[:, 24:32]
        b1c = bcol[:, 32:64]
        b2c = bcol[:, 64:72]
        quarter_col = bcol[:, 72:73]


        PSA = E(nc.psum_tensor("PSA", [128, 2048], F32))
        PSB = E(nc.psum_tensor("PSB", [128, 2048], F32))
        PSA_bf = PSA.ap().bitcast(BF16)
        PSB_bf = PSB.ap().bitcast(BF16)

        sLD = E(nc.semaphore("sLD"))
        sGP = E(nc.semaphore("sGP"))
        sPE = E(nc.semaphore("sPE"))
        sDV = E(nc.semaphore("sDV"))
        sAC = E(nc.semaphore("sAC"))
        sCC = E(nc.semaphore("sCC"))
        sST = E(nc.semaphore("sST"))
        block = E(nc.Block())
        C = Ctr()


        @block.sync
        def _(s):
            s.dma_start(out=identb_t[:, :], in_=identb[:, :]).then_inc(sLD, 16)
            s.dma_start(out=ident32_t[:, :], in_=ident32[:, :]).then_inc(sLD, 16)
            s.dma_start(out=ones128_t[:, :], in_=ones128[:, :]).then_inc(sLD, 16)
            s.dma_start(out=onescol_t[:, :], in_=onescol_b[:, :]).then_inc(sLD, 16)
            s.dma_start(out=V(A1, 0, F32, NT * D).rearrange("p (t d) -> p t d", t=NT),
                        in_=x_in.ap().rearrange("(t p) d -> p t d", p=128)).then_inc(sLD, 16)
            with nc.allow_non_contiguous_dma(reason="tiny bias col loads"):
                for i, nm in enumerate(("q", "k", "v", "o")):
                    s.dma_start(out=bcol[:, i * 8:(i + 1) * 8],
                                in_=ab[nm].ap().rearrange("(c p) -> p c", p=128),
                                allow_slow_non_contiguous=True).then_inc(sLD, 16)
                s.dma_start(out=b1c, in_=b1.ap().rearrange("(c p) -> p c", p=128),
                            allow_slow_non_contiguous=True).then_inc(sLD, 16)
                s.dma_start(out=b2c, in_=b2.ap().rearrange("(c p) -> p c", p=128),
                            allow_slow_non_contiguous=True).then_inc(sLD, 16)
            s.dma_start(out=V(A6, 0, F32, D)[0:1, :],
                        in_=l0ws.ap().unsqueeze(0)).then_inc(sLD, 16)
            s.dma_start(out=V(A6, 4096, F32, D)[0:1, :],
                        in_=l0fb.ap().unsqueeze(0)).then_inc(sLD, 16)
        C.inc("LD", 16 * 13)
        n_init_ld = C.get("LD")

        w8stage = V(A2, 0, BF16, DC * D)
        W8T = V(A5, 0, BF16, DC * D)
        WS_bc = V(A5, 16384, F32, D)
        B0H_bc = V(A5, 20480, F32, D)
        @block.gpsimd
        def _(g):
            g.dma_start(out=w8stage.rearrange("p (c d) -> p c d", c=DC),
                        in_=l0w8.ap().rearrange("(c p) d -> p c d", p=128)).then_inc(sGP, 16)
        C.inc("GP", 16)
        w8_gp = C.get("GP")

        @block.vector
        def _(v):
            v.wait_ge(sLD, n_init_ld)
            v.memset(quarter_col, 0.25)
            v.tensor_scalar(out=b1c, in0=b1c, scalar1=256.0, scalar2=None, op0=ALU.mult)
            v.tensor_scalar(out=b2c, in0=b2c, scalar1=256.0, scalar2=None, op0=ALU.mult)
            v.tensor_scalar(out=V(A6, 4096, F32, D)[0:1, :], in0=V(A6, 4096, F32, D)[0:1, :],
                            scalar1=0.5, scalar2=None, op0=ALU.mult).then_inc(sDV, 1)
        C.inc("DV", 1)
        p0_dv = C.get("DV")

        @block.tensor
        def _(t):
            t.wait_ge(sDV, p0_dv)
            for half in range(2):
                mm = t.matmul(PSA[:, half * 512:(half + 1) * 512], ones128_t[0:1, :],
                              V(A6, 0, F32, D)[0:1, half * 512:(half + 1) * 512],
                              start=True, stop=True)
                mm = t.matmul(PSA[:, 1024 + half * 512:1024 + (half + 1) * 512], ones128_t[0:1, :],
                              V(A6, 4096, F32, D)[0:1, half * 512:(half + 1) * 512],
                              start=True, stop=True)
            mm.then_inc(sPE, 1)
        C.inc("PE", 1)
        p0_pe = C.get("PE")

        @block.vector
        def _(v):
            v.wait_ge(sPE, p0_pe)
            v.tensor_copy(WS_bc, PSA[:, 0:1024])
            v.tensor_copy(B0H_bc, PSA[:, 1024:2048]).then_inc(sDV, 1)
        C.inc("DV", 1)

        w8_dv0 = C.get("DV")
        for grp in range(4):
            @block.tensor
            def _(t, grp=grp):
                if grp == 0:
                    t.wait_ge(sGP, w8_gp)
                    t.wait_ge(sDV, C.get("DV"))
                else:
                    t.wait_ge(sDV, C.get("DV"))
                for k in range(16):
                    idx = grp * 16 + k
                    c, j = idx // DC, idx % DC
                    mm = t.transpose(PSB_bf[:, k * 128:(k + 1) * 128],
                                     w8stage[:, c * D + j * 128: c * D + (j + 1) * 128],
                                     identb_t[:, :])
                    if k == 15:
                        mm.then_inc(sPE, 1)
            C.inc("PE", 1)

            @block.vector
            def _(v, grp=grp):
                v.wait_ge(sPE, C.get("PE"))
                last = None
                for k in range(16):
                    idx = grp * 16 + k
                    c, j = idx // DC, idx % DC
                    last = v.tensor_copy(W8T[:, j * D + c * 128: j * D + (c + 1) * 128],
                                         PSB_bf[:, k * 128:(k + 1) * 128])
                last.then_inc(sDV, 1)
            C.inc("DV", 1)
        w8t_dv = C.get("DV")

        xres = V(A1, 0, F32, NT * D)

        u_f32 = V(A6, 0, F32, D)
        sm_i32 = V(A6, 6144, I32, D)
        ub_bf = V(A6, 4096, BF16, D)
        uT_stage = V(A6, 6144, BF16, DC * 512)
        ep_f32 = V(A6, 6144, F32, D)

        def emit_ln(v, src, dst):
            v.tensor_reduce(ln_sc[:, 0:1], src, mybir.AxisListType.X, ALU.add)
            v.tensor_tensor(out=dst, in0=src, in1=src, op=ALU.mult)
            v.tensor_reduce(ln_sc[:, 1:2], dst, mybir.AxisListType.X, ALU.add)
            v.drain()
            v.tensor_scalar(out=ln_sc[:, 2:3], in0=ln_sc[:, 0:1], scalar1=1.0 / D,
                            scalar2=None, op0=ALU.mult)
            v.drain()
            v.tensor_tensor(out=ln_sc[:, 9:10], in0=ln_sc[:, 2:3], in1=ln_sc[:, 2:3], op=ALU.mult)
            v.tensor_scalar(out=ln_sc[:, 3:4], in0=ln_sc[:, 1:2], scalar1=1.0 / D,
                            scalar2=None, op0=ALU.mult)
            v.drain()
            v.tensor_tensor(out=ln_sc[:, 3:4], in0=ln_sc[:, 3:4], in1=ln_sc[:, 9:10], op=ALU.subtract)
            v.drain()
            v.tensor_scalar(out=ln_sc[:, 3:4], in0=ln_sc[:, 3:4], scalar1=LN_EPS,
                            scalar2=None, op0=ALU.add)
            v.drain()
            return v.reciprocal(ln_sc[:, 9:10], ln_sc[:, 3:4])

        def emit_rsqrt_refine(v, rcol, acol, t1, t2):
            v.tensor_tensor(out=t1, in0=rcol, in1=rcol, op=ALU.mult)
            v.drain()
            v.tensor_tensor(out=t1, in0=t1, in1=acol, op=ALU.mult)
            v.drain()
            v.tensor_scalar(out=t1, in0=t1, scalar1=-0.5, scalar2=1.5,
                            op0=ALU.mult, op1=ALU.add)
            v.drain()
            v.tensor_tensor(out=rcol, in0=rcol, in1=t1, op=ALU.mult)
            v.drain()

        def l0ffn_phase(tag):
            import sys
            dbg = globals().get("DBG_STEP", "full")
            steps = ["ln", "quant", "transpose", "mm", "ep1", "ep2", "ep3", "full"].index(dbg)
            for i in range(NT):
                xt = xres[:, i * D:(i + 1) * D]
                @block.vector
                def _(v, i=i, xt=xt):
                    if i > 0:
                        v.wait_ge(sPE, C.get("PE"))
                    emit_ln(v, xt, u_f32).then_inc(sDV, 1)
                C.inc("DV", 1)

                if steps < 1:
                    continue
                @block.scalar
                def _(sc):
                    sc.wait_ge(sDV, C.get("DV"))
                    sc.activation(ln_sc[:, 4:5], ln_sc[:, 9:10], AF.Sqrt).then_inc(sAC, 1)
                C.inc("AC", 1)

                @block.vector
                def _(v, xt=xt):
                    v.wait_ge(sAC, C.get("AC"))
                    emit_rsqrt_refine(v, ln_sc[:, 4:5], ln_sc[:, 3:4],
                                      ln_sc[:, 9:10], ln_sc[:, 10:11])
                    v.tensor_scalar(out=u_f32, in0=xt, scalar1=ln_sc[:, 2:3],
                                    scalar2=ln_sc[:, 4:5], op0=ALU.subtract, op1=ALU.mult)
                    v.reduce_max(ln_sc[:, 5:6], u_f32, axis=mybir.AxisListType.X,
                                 apply_absolute_value=True)
                    v.drain()
                    v.tensor_scalar(out=ln_sc[:, 6:7], in0=ln_sc[:, 5:6],
                                    scalar1=L0FFN_EPS, scalar2=1.0 / 127.0,
                                    op0=ALU.max, op1=ALU.mult)
                    v.drain()
                    v.reciprocal(ln_sc[:, 7:8], ln_sc[:, 6:7])
                    v.tensor_scalar(out=ln_sc[:, 8:9], in0=ln_sc[:, 6:7],
                                    scalar1=0.5, scalar2=None, op0=ALU.mult)
                    v.drain()
                    v.tensor_scalar(out=sm_i32, in0=u_f32, scalar1=ln_sc[:, 7:8],
                                    scalar2=None, op0=ALU.mult)
                    v.drain()
                    v.tensor_copy(ub_bf, sm_i32).then_inc(sDV, 1)
                C.inc("DV", 1)
                q_dv = C.get("DV")

                if steps < 2:
                    continue
                @block.tensor
                def _(t, i=i, q_dv=q_dv):
                    t.wait_ge(sDV, q_dv)
                    for j in range(DC):
                        mm = t.transpose(PSB_bf[:, j * 128:(j + 1) * 128],
                                         ub_bf[:, j * 128:(j + 1) * 128],
                                         identb_t[:, :])
                    mm.then_inc(sPE, 1)
                C.inc("PE", 1)
                tp_pe = C.get("PE")

                @block.vector
                def _(v, tp_pe=tp_pe):
                    v.wait_ge(sPE, tp_pe)
                    v.tensor_copy(uT_stage[:, 0:1024], PSB_bf[:, 0:1024]).then_inc(sDV, 1)
                C.inc("DV", 1)
                ev_dv = C.get("DV")

                if steps < 3:
                    continue
                @block.tensor
                def _(t, ev_dv=ev_dv):
                    t.wait_ge(sDV, ev_dv)
                    for j in range(DC):
                        lhsT = uT_stage[:, j * 128:(j + 1) * 128]
                        last = j == DC - 1
                        mm0 = t.matmul(PSA[:, 0:512], lhsT, W8T[:, j * D:j * D + 512],
                                       start=(j == 0), stop=last)
                        mm1 = t.matmul(PSA[:, 512:1024], lhsT, W8T[:, j * D + 512:(j + 1) * D],
                                       start=(j == 0), stop=last)
                        if last:
                            mm1.then_inc(sPE, 1)
                C.inc("PE", 1)
                mm_pe = C.get("PE")

                if steps < 4:
                    continue
                @block.vector
                def _(v, xt=xt, mm_pe=mm_pe, steps=steps):
                    v.wait_ge(sPE, mm_pe)
                    last = v.tensor_scalar(out=ep_f32, in0=PSA[:, 0:1024],
                                           scalar1=ln_sc[:, 8:9], scalar2=None, op0=ALU.mult)
                    if steps >= 5:
                        last = v.tensor_tensor(out=ep_f32, in0=ep_f32, in1=WS_bc, op=ALU.mult)
                    if steps >= 6:
                        last = v.tensor_tensor(out=ep_f32, in0=ep_f32, in1=B0H_bc, op=ALU.add)
                    if steps >= 7:
                        last = v.tensor_tensor(out=xt, in0=xt, in1=ep_f32, op=ALU.add)
                    last.then_inc(sDV, 1)
                C.inc("DV", 1)


        def tern_prep():
            tstage = V(A5, 24576, F32, 1024)
            tstage_i = V(A5, 28672, I32, 1024)
            tbf = V(A6, 4096, BF16, D)
            for mi, nm in enumerate(("q", "k", "v", "o")):
                @block.vector
                def _(v, mi=mi):
                    v.wait_ge(sGP, C.get("GP"))
                    v.memset(ln_sc[:, 12:13], 0.0)
                    v.drain()
                C.inc("DV", 0)
                for st in range(8):
                    @block.sync
                    def _(s, nm=nm, st=st):
                        s.wait_ge(sDV, C.get("DV"))
                        s.dma_start(out=tstage,
                                    in_=bw[nm].ap()[st * 128:(st + 1) * 128, :]).then_inc(sLD, 16)
                    C.inc("LD", 16)
                    @block.vector
                    def _(v):
                        v.wait_ge(sLD, C.get("LD"))
                        v.tensor_reduce(ln_sc[:, 11:12], tstage, mybir.AxisListType.X,
                                        ALU.add, apply_absolute_value=True)
                        v.drain()
                        v.tensor_tensor(out=ln_sc[:, 12:13], in0=ln_sc[:, 12:13],
                                        in1=ln_sc[:, 11:12], op=ALU.add)
                        v.drain()
                        v.engine_nop().then_inc(sDV, 1)
                    C.inc("DV", 1)
                @block.tensor
                def _(t):
                    t.wait_ge(sDV, C.get("DV"))
                    t.matmul(PSB[:, 0:1], ones128_t[:, :], ln_sc[:, 12:13],
                             start=True, stop=True).then_inc(sPE, 1)
                C.inc("PE", 1)
                @block.vector
                def _(v, mi=mi):
                    v.wait_ge(sPE, C.get("PE"))
                    v.tensor_scalar(out=ln_sc[:, 13:14], in0=PSB[:, 0:1],
                                    scalar1=1.0 / (D * D), scalar2=WQ_EPS,
                                    op0=ALU.mult, op1=ALU.max)
                    v.drain()
                    v.tensor_copy(swinv[:, mi:mi + 1], ln_sc[:, 13:14])
                    v.reciprocal(swrec[:, mi:mi + 1], ln_sc[:, 13:14])
                    v.drain()
                    v.engine_nop().then_inc(sDV, 1)
                C.inc("DV", 1)
                wqT_dst = V(A2, mi * 16384, BF16, DC * D)
                for er in range(8):
                    @block.sync
                    def _(s, nm=nm, er=er):
                        s.wait_ge(sDV, C.get("DV"))
                        s.dma_start(out=tstage,
                                    in_=bw[nm].ap()[er * 128:(er + 1) * 128, 0:1024]).then_inc(sLD, 16)
                    C.inc("LD", 16)
                    @block.vector
                    def _(v, mi=mi):
                        v.wait_ge(sLD, C.get("LD"))
                        v.wait_ge(sPE, C.get("PE"))
                        v.tensor_scalar(out=tstage_i, in0=tstage, scalar1=swrec[:, mi:mi + 1],
                                        scalar2=None, op0=ALU.mult)
                        v.drain()
                        v.tensor_scalar(out=tbf, in0=tstage_i, scalar1=-1.0, scalar2=1.0,
                                        op0=ALU.max, op1=ALU.min)
                        v.engine_nop().then_inc(sDV, 1)
                    C.inc("DV", 1)
                    @block.tensor
                    def _(t):
                        t.wait_ge(sDV, C.get("DV"))
                        for j in range(DC):
                            mm = t.transpose(PSB_bf[:, j * 128:(j + 1) * 128],
                                             tbf[:, j * 128:(j + 1) * 128], identb_t[:, :])
                        mm.then_inc(sPE, 1)
                    C.inc("PE", 1)
                    @block.vector
                    def _(v, er=er, wqT_dst=wqT_dst):
                        v.wait_ge(sPE, C.get("PE"))
                        dst = wqT_dst.rearrange("p (dj e) -> p dj e", dj=DC)[:, :, er * 128:(er + 1) * 128]
                        v.tensor_copy(dst, PSB_bf[:, 0:1024].rearrange("p (dj e) -> p dj e", dj=DC)).then_inc(sDV, 1)
                    C.inc("DV", 1)
                @block.sync
                def _(s, nm=nm, wqT_dst=wqT_dst):
                    s.wait_ge(sDV, C.get("DV"))
                    s.dma_start(out=tern_sp[nm].ap(), in_=wqT_dst).then_inc(sST, 16)
                C.inc("ST", 16)

        def l0attn_wprep():
            tbf = V(A6, 4096, BF16, D)
            for mi, nm in enumerate(("q", "k", "v", "o")):
                wT_dst = V(A2, mi * 16384, BF16, DC * D)
                for er in range(8):
                    @block.gpsimd
                    def _(g, nm=nm, er=er):
                        g.wait_ge(sPE, C.get("PE"))
                        g.dma_start(out=tbf,
                                    in_=aw[nm].ap()[er * 128:(er + 1) * 128, :]).then_inc(sGP, 16)
                    C.inc("GP", 16)
                    @block.tensor
                    def _(t):
                        t.wait_ge(sGP, C.get("GP"))
                        t.wait_ge(sDV, C.get("DV"))
                        for j in range(DC):
                            mm = t.transpose(PSB_bf[:, j * 128:(j + 1) * 128],
                                             tbf[:, j * 128:(j + 1) * 128], identb_t[:, :])
                        mm.then_inc(sPE, 1)
                    C.inc("PE", 1)
                    @block.vector
                    def _(v, er=er, wT_dst=wT_dst):
                        v.wait_ge(sPE, C.get("PE"))
                        v.wait_ge(sST, C.get("ST"))
                        dst = wT_dst.rearrange("p (dj e) -> p dj e", dj=DC)[:, :, er * 128:(er + 1) * 128]
                        v.tensor_copy(dst, PSB_bf[:, 0:1024].rearrange("p (dj e) -> p dj e", dj=DC)).then_inc(sDV, 1)
                    C.inc("DV", 1)

        def emit_tile_ln(xt):
            @block.vector
            def _(v, xt=xt):
                v.wait_ge(sPE, C.get("PE"))
                emit_ln(v, xt, u_f32).then_inc(sDV, 1)
            C.inc("DV", 1)
            @block.scalar
            def _(sc):
                sc.wait_ge(sDV, C.get("DV"))
                sc.activation(ln_sc[:, 4:5], ln_sc[:, 9:10], AF.Sqrt).then_inc(sAC, 1)
            C.inc("AC", 1)
            @block.vector
            def _(v, xt=xt):
                v.wait_ge(sAC, C.get("AC"))
                emit_rsqrt_refine(v, ln_sc[:, 4:5], ln_sc[:, 3:4],
                                  ln_sc[:, 9:10], ln_sc[:, 10:11])
                v.tensor_scalar(out=u_f32, in0=xt, scalar1=ln_sc[:, 2:3],
                                scalar2=ln_sc[:, 4:5], op0=ALU.subtract, op1=ALU.mult).then_inc(sDV, 1)
            C.inc("DV", 1)

        def emit_transpose_to_stage(src_bf, tg):
            @block.tensor
            def _(t, src_bf=src_bf):
                t.wait_ge(sDV, C.get("DV"))
                for j in range(DC):
                    mm = t.transpose(PSB_bf[:, 2048 + j * 128:2048 + (j + 1) * 128],
                                     src_bf[:, j * 128:(j + 1) * 128], identb_t[:, :])
                mm.then_inc(sPE, 1)
            C.inc("PE", 1)
            @block.vector
            def _(v, tg=tg):
                v.wait_ge(sPE, C.get("PE"))
                dst = uT_stage.rearrange("p (j t) -> p j t", j=DC)[:, :, tg * 128:(tg + 1) * 128]
                v.tensor_copy(dst, PSB_bf[:, 2048:3072].rearrange("p (j t) -> p j t", j=DC)).then_inc(sDV, 1)
            C.inc("DV", 1)

        def l0attn_phase():
            WT = {nm: V(A2, mi * 16384, BF16, DC * D)
                  for mi, nm in enumerate(("q", "k", "v", "o"))}
            qTsb = V(A4, 0, BF16, DC * TL)
            K_loc = V(A6, 0, BF16, D)
            V_loc = V(A6, 2048, BF16, D)
            fmA = V(A6, 4096, F32, 512)
            BKb = V(A5, 24576, F32, D)
            BVb = V(A5, 28672, F32, D)
            attnN = V(A5, 24576, BF16, DC * 512)
            tmp64 = V(A6, 4096, F32, 512)

            @block.sync
            def _(s):
                s.wait_ge(sDV, C.get("DV"))
                s.dma_start(out=V(A6, 0, F32, D)[0:1, :], in_=ab["k"].ap().unsqueeze(0)).then_inc(sLD, 16)
                s.dma_start(out=V(A6, 4096, F32, D)[0:1, :], in_=ab["v"].ap().unsqueeze(0)).then_inc(sLD, 16)
            C.inc("LD", 32)
            @block.tensor
            def _(t):
                t.wait_ge(sLD, C.get("LD"))
                for half in range(2):
                    t.matmul(PSB[:, half * 512:(half + 1) * 512], ones128_t[0:1, :],
                             V(A6, 0, F32, D)[0:1, half * 512:(half + 1) * 512],
                             start=True, stop=True)
                    mm = t.matmul(PSB[:, 1024 + half * 512:1024 + (half + 1) * 512], ones128_t[0:1, :],
                                  V(A6, 4096, F32, D)[0:1, half * 512:(half + 1) * 512],
                                  start=True, stop=True)
                mm.then_inc(sPE, 1)
            C.inc("PE", 1)
            @block.vector
            def _(v):
                v.wait_ge(sPE, C.get("PE"))
                v.tensor_copy(BKb, PSB[:, 0:1024])
                v.tensor_copy(BVb, PSB[:, 1024:2048]).then_inc(sDV, 1)
            C.inc("DV", 1)

            for g in range(NG):
                for tg in range(4):
                    i = g * 4 + tg
                    xt = xres[:, i * D:(i + 1) * D]
                    emit_tile_ln(xt)
                    @block.vector
                    def _(v):
                        v.tensor_copy(ub_bf, u_f32).then_inc(sDV, 1)
                    C.inc("DV", 1)
                    emit_transpose_to_stage(ub_bf, tg)
                for ec in range(DC):
                    @block.tensor
                    def _(t, ec=ec):
                        t.wait_ge(sDV, C.get("DV"))
                        for j in range(DC):
                            mm = t.matmul(PSA[:, 0:512],
                                          WT["q"][:, j * D + ec * 128:j * D + (ec + 1) * 128],
                                          uT_stage[:, j * 512:(j + 1) * 512],
                                          start=(j == 0), stop=(j == DC - 1))
                        mm.then_inc(sPE, 1)
                    C.inc("PE", 1)
                    @block.vector
                    def _(v, ec=ec):
                        v.wait_ge(sPE, C.get("PE"))
                        v.tensor_scalar(out=fmA, in0=PSA[:, 0:512],
                                        scalar1=bq_col[:, ec:ec + 1], scalar2=None, op0=ALU.add)
                        v.tensor_scalar(out=PSB[:, 1536:2048], in0=fmA, scalar1=0.0,
                                        scalar2=None, op0=ALU.max)
                        v.tensor_scalar(out=PSB[:, 1024:1536], in0=fmA, scalar1=0.0,
                                        scalar2=None, op0=ALU.min).then_inc(sDV, 1)
                    C.inc("DV", 1)
                    @block.scalar
                    def _(sc):
                        sc.wait_ge(sDV, C.get("DV"))
                        sc.activation(fmA, PSB[:, 1024:1536], AF.Exp).then_inc(sAC, 1)
                    C.inc("AC", 1)
                    @block.vector
                    def _(v, ec=ec, g=g):
                        v.wait_ge(sAC, C.get("AC"))
                        v.tensor_tensor(out=qTsb[:, ec * TL + g * 512: ec * TL + g * 512 + 512],
                                        in0=fmA, in1=PSB[:, 1536:2048], op=ALU.add).then_inc(sDV, 1)
                    C.inc("DV", 1)
                for tg in range(4):
                    i = g * 4 + tg
                    for which in ("k", "v"):
                        @block.tensor
                        def _(t, tg=tg, which=which):
                            t.wait_ge(sDV, C.get("DV"))
                            for j in range(DC):
                                lhsT = uT_stage[:, j * 512 + tg * 128: j * 512 + (tg + 1) * 128]
                                mm0 = t.matmul(PSB[:, 0:512], lhsT,
                                               WT[which][:, j * D:j * D + 512],
                                               start=(j == 0), stop=(j == DC - 1))
                                mm1 = t.matmul(PSB[:, 512:1024], lhsT,
                                               WT[which][:, j * D + 512:(j + 1) * D],
                                               start=(j == 0), stop=(j == DC - 1))
                            mm1.then_inc(sPE, 1)
                        C.inc("PE", 1)
                        if which == "k":
                            for half in range(2):
                                @block.vector
                                def _(v, half=half):
                                    v.wait_ge(sPE, C.get("PE"))
                                    v.tensor_tensor(out=fmA, in0=PSB[:, half * 512:(half + 1) * 512],
                                                    in1=BKb[:, half * 512:(half + 1) * 512], op=ALU.add)
                                    v.tensor_scalar(out=PSB[:, 1536:2048], in0=fmA, scalar1=0.0,
                                                    scalar2=None, op0=ALU.max)
                                    v.tensor_scalar(out=PSB[:, 1024:1536], in0=fmA, scalar1=0.0,
                                                    scalar2=None, op0=ALU.min).then_inc(sDV, 1)
                                C.inc("DV", 1)
                                @block.scalar
                                def _(sc):
                                    sc.wait_ge(sDV, C.get("DV"))
                                    sc.activation(fmA, PSB[:, 1024:1536], AF.Exp).then_inc(sAC, 1)
                                C.inc("AC", 1)
                                @block.vector
                                def _(v, half=half):
                                    v.wait_ge(sAC, C.get("AC"))
                                    v.tensor_tensor(out=K_loc[:, half * 512:(half + 1) * 512],
                                                    in0=fmA, in1=PSB[:, 1536:2048], op=ALU.add).then_inc(sDV, 1)
                                C.inc("DV", 1)
                        else:
                            @block.vector
                            def _(v):
                                v.wait_ge(sPE, C.get("PE"))
                                v.tensor_tensor(out=V_loc, in0=PSB[:, 0:1024], in1=BVb,
                                                op=ALU.add).then_inc(sDV, 1)
                            C.inc("DV", 1)
                    @block.tensor
                    def _(t, i=i):
                        t.wait_ge(sDV, C.get("DV"))
                        for h in range(H):
                            hp, rs = h // 2, (h % 2) * 64
                            t.matmul(PSA[rs:rs + 64, 1024 + hp * 64:1024 + (hp + 1) * 64],
                                     K_loc[:, h * 64:(h + 1) * 64], V_loc[:, h * 64:(h + 1) * 64],
                                     start=(i == 0 and h == 0), stop=(i == NT - 1 and h == H - 1),
                                     skip_group_check=True)
                            mm = t.matmul(PSA[rs:rs + 64, 1536 + hp:1536 + hp + 1],
                                          K_loc[:, h * 64:(h + 1) * 64], onescol_t[:, :],
                                          start=(i == 0 and h == 0), stop=(i == NT - 1 and h == H - 1),
                                          skip_group_check=True)
                        mm.then_inc(sPE, 1)
                    C.inc("PE", 1)
            kv_sb = V(A6, 6144, F32, 520)
            @block.vector
            def _(v):
                v.wait_ge(sPE, C.get("PE"))
                v.tensor_copy(kv_sb[:, 0:512], PSA[:, 1024:1536])
                v.tensor_copy(kv_sb[:, 512:520], PSA[:, 1536:1544])
                v.drain()
                v.engine_nop().then_inc(sDV, 1)
            C.inc("DV", 1)
            @block.gpsimd
            def _(g_):
                g_.wait_ge(sDV, C.get("DV"))
                g_.dma_start(out=kv_in.ap()[:, 0:520], in_=kv_sb).then_inc(sGP, 16)
                g_.wait_ge(sGP, C.inc("GP", 16))
                g_.collective_compute(
                    "AllReduce", ALU.add, replica_groups=[list(range(NCORE))],
                    ins=[kv_in.ap().opt()], outs=[kv_out.ap().opt()]).then_inc(sCC, 1)
            C.inc("CC", 1)
            @block.sync
            def _(s):
                s.wait_ge(sCC, C.get("CC"))
                s.dma_start(out=kv_sb, in_=kv_out.ap()[:, 0:520]).then_inc(sLD, 16)
            C.inc("LD", 16)
            kvx = V(A6, 8224, BF16, 520)
            @block.vector
            def _(v):
                v.wait_ge(sLD, C.get("LD"))
                kvx3 = kvx.rearrange("p (hp c) -> p hp c", hp=8)
                v.tensor_copy(kvx3[:, :, 0:64], kv_sb[:, 0:512].rearrange("p (hp c) -> p hp c", hp=8))
                v.tensor_copy(kvx3[:, :, 64:65], kv_sb[:, 512:520].rearrange("p (hp c) -> p hp c", hp=8)).then_inc(sDV, 1)
            C.inc("DV", 1)
            for qb in range(4):
                for h in range(H):
                    hp, rs = h // 2, (h % 2) * 64
                    @block.tensor
                    def _(t, hp=hp, rs=rs, h=h, qb=qb):
                        t.wait_ge(sDV, C.get("DV"))
                        t.matmul(PSB[0:65, 0:512], kvx[rs:rs + 64, hp * 65:(hp + 1) * 65],
                                 qTsb[rs:rs + 64, (h // 2) * TL + qb * 512:(h // 2) * TL + qb * 512 + 512],
                                 start=True, stop=True).then_inc(sPE, 1)
                    C.inc("PE", 1)
                    @block.vector
                    def _(v):
                        v.wait_ge(sPE, C.get("PE"))
                        v.tensor_scalar(out=V(A6, 9472, F32, 512)[0:1, :], in0=PSB[64:65, 0:512],
                                        scalar1=L0ATTN_EPS, scalar2=None, op0=ALU.add)
                        v.drain()
                        v.reciprocal(V(A6, 11520, F32, 512)[0:1, :], V(A6, 9472, F32, 512)[0:1, :])
                        v.drain()
                        v.engine_nop().then_inc(sDV, 1)
                    C.inc("DV", 1)
                    @block.tensor
                    def _(t):
                        t.wait_ge(sDV, C.get("DV"))
                        t.matmul(PSB[0:64, 512:1024], ones128_t[0:1, 0:64],
                                 V(A6, 11520, F32, 512)[0:1, :], start=True, stop=True).then_inc(sPE, 1)
                    C.inc("PE", 1)
                    @block.vector
                    def _(v, h=h, rs=rs):
                        v.wait_ge(sPE, C.get("PE"))
                        v.tensor_copy(tmp64[0:64, :], PSB[0:64, 0:512])
                        v.tensor_tensor(out=attnN[rs:rs + 64, (h // 2) * 512:(h // 2) * 512 + 512],
                                        in0=tmp64[0:64, :], in1=PSB[0:64, 512:1024],
                                        op=ALU.mult).then_inc(sDV, 1)
                    C.inc("DV", 1)
                for oc in range(DC):
                    @block.tensor
                    def _(t, oc=oc):
                        t.wait_ge(sDV, C.get("DV"))
                        for j in range(DC):
                            mm = t.matmul(PSA[:, 0:512],
                                          WT["o"][:, j * D + oc * 128:j * D + (oc + 1) * 128],
                                          attnN[:, j * 512:(j + 1) * 512],
                                          start=(j == 0), stop=(j == DC - 1))
                        mm.then_inc(sPE, 1)
                    C.inc("PE", 1)
                    @block.vector
                    def _(v, oc=oc):
                        v.wait_ge(sPE, C.get("PE"))
                        v.tensor_scalar(out=V(A6, 9472, F32, 512), in0=PSA[:, 0:512],
                                        scalar1=bo_col[:, oc:oc + 1], scalar2=None,
                                        op0=ALU.add).then_inc(sDV, 1)
                    C.inc("DV", 1)
                    @block.tensor
                    def _(t):
                        t.wait_ge(sDV, C.get("DV"))
                        for sub in range(4):
                            mm = t.transpose(PSB[:, 512 + sub * 128:512 + (sub + 1) * 128],
                                             V(A6, 9472, F32, 512)[:, sub * 128:(sub + 1) * 128],
                                             ident32_t[:, :])
                        mm.then_inc(sPE, 1)
                    C.inc("PE", 1)
                    @block.vector
                    def _(v, qb=qb, oc=oc):
                        v.wait_ge(sPE, C.get("PE"))
                        xr3 = xres.rearrange("p (t d) -> p t d", t=NT)
                        dst = xr3[:, qb * 4:(qb + 1) * 4, oc * 128:(oc + 1) * 128]
                        v.tensor_tensor(out=dst, in0=dst,
                                        in1=PSB[:, 512:1024].rearrange("p (s d) -> p s d", s=4),
                                        op=ALU.add).then_inc(sDV, 1)
                    C.inc("DV", 1)

        def l1ffn_phase():
            w2sb = V(A2, 0, BF16, FC * D)
            w1buf = [V(A5, 0, BF16, DC * 128), V(A5, 2048, BF16, DC * 128)]
            hqT = V(A4, 0, BF16, FC * 512)
            P1 = V(A6, 0, F32, 512)
            P2 = V(A6, 2048, F32, 512)
            P2i = V(A6, 2048, I32, 512)
            P3f = V(A6, 4096, F32, 512)
            P3i = V(A6, 4096, I32, 512)
            @block.gpsimd
            def _(g_):
                g_.wait_ge(sDV, C.get("DV"))
                g_.wait_ge(sPE, C.get("PE"))
                g_.dma_start(out=w2sb.rearrange("p (j o) -> p j o", j=FC),
                             in_=w2.ap().rearrange("(j p) o -> p j o", p=128)).then_inc(sGP, 16)
            C.inc("GP", 16)
            for g in range(NG):
                for tg in range(4):
                    i = g * 4 + tg
                    xt = xres[:, i * D:(i + 1) * D]
                    emit_tile_ln(xt)
                    @block.vector
                    def _(v):
                        v.tensor_copy(ub_bf, u_f32).then_inc(sDV, 1)
                    C.inc("DV", 1)
                    emit_transpose_to_stage(ub_bf, tg)
                for fc in range(FC):
                    par = fc % 2
                    @block.gpsimd
                    def _(g_, fc=fc, par=par):
                        g_.wait_ge(sPE, C.get("PE"))
                        g_.dma_start(out=w1buf[par].rearrange("p (j f) -> p j f", j=DC),
                                     in_=w1.ap()[:, fc * 128:(fc + 1) * 128]
                                     .rearrange("(j p) f -> p j f", p=128)).then_inc(sGP, 16)
                    C.inc("GP", 16)
                    gp_after_load = C.get("GP")
                    bank = fc % 2
                    @block.tensor
                    def _(t, fc=fc, par=par, bank=bank, gp_after_load=gp_after_load):
                        t.wait_ge(sGP, gp_after_load)
                        t.wait_ge(sDV, C.get("DV"))
                        for j in range(DC):
                            mm = t.matmul(PSA[:, bank * 512:(bank + 1) * 512],
                                          w1buf[par][:, j * 128:(j + 1) * 128],
                                          uT_stage[:, j * 512:(j + 1) * 512],
                                          start=(j == 0), stop=(j == DC - 1))
                        mm.then_inc(sPE, 1)
                    C.inc("PE", 1)
                    pe_h = C.get("PE")
                    @block.vector
                    def _(v, fc=fc, bank=bank, pe_h=pe_h):
                        v.wait_ge(sPE, pe_h)
                        v.tensor_scalar(out=P3i, in0=PSA[:, bank * 512:(bank + 1) * 512],
                                        scalar1=256.0, scalar2=b1c[:, fc:fc + 1],
                                        op0=ALU.mult, op1=ALU.add)
                        v.drain()
                        v.tensor_scalar(out=P1, in0=P3i, scalar1=1.0 / 64, scalar2=None,
                                        op0=ALU.mult)
                        v.drain()
                        v.tensor_scalar(out=P2i, in0=P1, scalar1=-0.5, scalar2=None,
                                        op0=ALU.add)
                        v.drain()
                        v.engine_nop().then_inc(sDV, 1)
                    C.inc("DV", 1)
                    @block.scalar
                    def _(sc):
                        sc.wait_ge(sDV, C.get("DV"))
                        sc.activation(PSB[:, 0:512], P2i, AF.Gelu, scale=0.25)
                        sc.activation(PSB[:, 512:1024], P2i, AF.Gelu, scale=0.25,
                                      bias=quarter_col).then_inc(sAC, 1)
                    C.inc("AC", 1)
                    @block.vector
                    def _(v, fc=fc):
                        v.wait_ge(sAC, C.get("AC"))
                        v.tensor_tensor(out=P3f, in0=P1, in1=P2i, op=ALU.subtract)
                        v.tensor_copy(P2, PSB[:, 0:512])
                        v.drain()
                        v.tensor_tensor(out=P1, in0=PSB[:, 512:1024], in1=P2, op=ALU.subtract)
                        v.drain()
                        v.tensor_tensor(out=PSB[:, 0:512], in0=P3f, in1=P1, op=ALU.mult)
                        v.drain()
                        v.tensor_tensor(out=P1, in0=PSB[:, 0:512], in1=P2, op=ALU.add)
                        v.drain()
                        v.tensor_scalar(out=P3i, in0=P1, scalar1=256.0, scalar2=None,
                                        op0=ALU.mult)
                        v.drain()
                        v.tensor_scalar(out=hqT[:, fc * 512:(fc + 1) * 512], in0=P3i,
                                        scalar1=1.0 / 256, scalar2=None,
                                        op0=ALU.mult).then_inc(sDV, 1)
                    C.inc("DV", 1)
                for oc in range(DC):
                    @block.tensor
                    def _(t, oc=oc):
                        t.wait_ge(sDV, C.get("DV"))
                        t.wait_ge(sGP, C.get("GP"))
                        for j2 in range(FC):
                            mm = t.matmul(PSA[:, 1024:1536],
                                          w2sb[:, j2 * D + oc * 128:j2 * D + (oc + 1) * 128],
                                          hqT[:, j2 * 512:(j2 + 1) * 512],
                                          start=(j2 == 0), stop=(j2 == FC - 1))
                        mm.then_inc(sPE, 1)
                    C.inc("PE", 1)
                    @block.vector
                    def _(v, oc=oc):
                        v.wait_ge(sPE, C.get("PE"))
                        v.tensor_scalar(out=P3i, in0=PSA[:, 1024:1536], scalar1=256.0,
                                        scalar2=b2c[:, oc:oc + 1], op0=ALU.mult, op1=ALU.add)
                        v.drain()
                        v.tensor_scalar(out=P1, in0=P3i, scalar1=1.0 / 512, scalar2=None,
                                        op0=ALU.mult).then_inc(sDV, 1)
                    C.inc("DV", 1)
                    @block.tensor
                    def _(t):
                        t.wait_ge(sDV, C.get("DV"))
                        for sub in range(4):
                            mm = t.transpose(PSB[:, 1024 + sub * 128:1024 + (sub + 1) * 128],
                                             P1[:, sub * 128:(sub + 1) * 128], ident32_t[:, :])
                        mm.then_inc(sPE, 1)
                    C.inc("PE", 1)
                    @block.vector
                    def _(v, g=g, oc=oc):
                        v.wait_ge(sPE, C.get("PE"))
                        xr3 = xres.rearrange("p (t d) -> p t d", t=NT)
                        dst = xr3[:, g * 4:(g + 1) * 4, oc * 128:(oc + 1) * 128]
                        v.tensor_tensor(out=dst, in0=dst,
                                        in1=PSB[:, 1024:1536].rearrange("p (s d) -> p s d", s=4),
                                        op=ALU.add).then_inc(sDV, 1)
                    C.inc("DV", 1)

        def l1attn_phase():
            WT = {nm: V(A2, mi * 16384, BF16, DC * D)
                  for mi, nm in enumerate(("q", "k", "v", "o"))}
            qTsb = V(A4, 0, BF16, DC * TL)
            kTsb = V(A5, 0, BF16, DC * TL)
            V_tiles = [V(A6, 0, BF16, H * 65), V(A6, 2176, BF16, H * 65)]
            for mi, nm in enumerate(("q", "k", "v", "o")):
                @block.sync
                def _(s, mi=mi, nm=nm):
                    s.wait_ge(sDV, C.get("DV"))
                    s.wait_ge(sPE, C.get("PE"))
                    s.dma_start(out=V(A2, mi * 16384, BF16, DC * D),
                                in_=tern_sp[nm].ap()).then_inc(sLD, 16)
            C.inc("LD", 16 * 4)
            ld_tern = C.get("LD")
            for g in range(NG):
                for tg in range(4):
                    i = g * 4 + tg
                    xt = xres[:, i * D:(i + 1) * D]
                    if g > 0 and tg == 0:
                        @block.vector
                        def _(v):
                            v.wait_ge(sGP, C.get("GP"))
                    emit_tile_ln(xt)
                    @block.vector
                    def _(v):
                        v.wait_ge(sPE, C.get("PE"))
                        PSB_f = PSB.ap()
                        v.tensor_tensor(out=PSB_f[:, 1024:1536], in0=u_f32[:, 0:512],
                                        in1=u_f32[:, 0:512], op=ALU.mult)
                        v.tensor_reduce(ln_sc[:, 11:12], PSB_f[:, 1024:1536],
                                        mybir.AxisListType.X, ALU.add)
                        v.tensor_tensor(out=PSB_f[:, 1024:1536], in0=u_f32[:, 512:1024],
                                        in1=u_f32[:, 512:1024], op=ALU.mult)
                        v.tensor_reduce(ln_sc[:, 14:15], PSB_f[:, 1024:1536],
                                        mybir.AxisListType.X, ALU.add)
                        v.drain()
                        v.tensor_tensor(out=ln_sc[:, 11:12], in0=ln_sc[:, 11:12],
                                        in1=ln_sc[:, 14:15], op=ALU.add)
                        v.drain()
                        v.tensor_scalar(out=ln_sc[:, 11:12], in0=ln_sc[:, 11:12],
                                        scalar1=1.0 / D, scalar2=RMS_EPS,
                                        op0=ALU.mult, op1=ALU.add)
                        v.drain()
                        v.reciprocal(ln_sc[:, 12:13], ln_sc[:, 11:12])
                        v.drain()
                        v.engine_nop().then_inc(sDV, 1)
                    C.inc("DV", 1)
                    @block.scalar
                    def _(sc):
                        sc.wait_ge(sDV, C.get("DV"))
                        sc.activation(ln_sc[:, 12:13], ln_sc[:, 12:13], AF.Sqrt).then_inc(sAC, 1)
                    C.inc("AC", 1)
                    @block.vector
                    def _(v):
                        v.wait_ge(sAC, C.get("AC"))
                        emit_rsqrt_refine(v, ln_sc[:, 12:13], ln_sc[:, 11:12],
                                          ln_sc[:, 9:10], ln_sc[:, 10:11])
                        v.reduce_max(ln_sc[:, 5:6], u_f32, axis=mybir.AxisListType.X,
                                     apply_absolute_value=True)
                        v.drain()
                        v.tensor_tensor(out=ln_sc[:, 5:6], in0=ln_sc[:, 5:6],
                                        in1=ln_sc[:, 12:13], op=ALU.mult)
                        v.drain()
                        v.tensor_scalar(out=ln_sc[:, 6:7], in0=ln_sc[:, 5:6],
                                        scalar1=AQ_EPS, scalar2=1.0 / 127.0,
                                        op0=ALU.max, op1=ALU.mult)
                        v.drain()
                        v.reciprocal(ln_sc[:, 7:8], ln_sc[:, 6:7])
                        v.drain()
                        v.tensor_tensor(out=ln_sc[:, 13:14], in0=ln_sc[:, 12:13],
                                        in1=ln_sc[:, 7:8], op=ALU.mult)
                        v.drain()
                        PSB_i = PSB.ap().bitcast(I32)
                        for hh in range(2):
                            v.tensor_scalar(out=PSB_i[:, 1536:2048], in0=u_f32[:, hh * 512:(hh + 1) * 512],
                                            scalar1=ln_sc[:, 13:14], scalar2=None, op0=ALU.mult)
                            v.drain()
                            v.tensor_scalar(out=ub_bf[:, hh * 512:(hh + 1) * 512], in0=PSB_i[:, 1536:2048],
                                            scalar1=ln_sc[:, 6:7], scalar2=None, op0=ALU.mult)
                            v.drain()
                        v.engine_nop().then_inc(sDV, 1)
                    C.inc("DV", 1)
                    emit_transpose_to_stage(ub_bf, tg)
                @block.tensor
                def _(t):
                    t.wait_ge(sLD, ld_tern)
                for dstsb, wname, sw in ((qTsb, "q", 0), (kTsb, "k", 1)):
                    for ec in range(DC):
                        @block.tensor
                        def _(t, wname=wname, ec=ec):
                            t.wait_ge(sDV, C.get("DV"))
                            for j in range(DC):
                                mm = t.matmul(PSA[:, 0:512],
                                              WT[wname][:, j * D + ec * 128:j * D + (ec + 1) * 128],
                                              uT_stage[:, j * 512:(j + 1) * 512],
                                              start=(j == 0), stop=(j == DC - 1))
                            mm.then_inc(sPE, 1)
                        C.inc("PE", 1)
                        @block.vector
                        def _(v, dstsb=dstsb, ec=ec, g=g, sw=sw):
                            v.wait_ge(sPE, C.get("PE"))
                            v.tensor_scalar(out=dstsb[:, ec * TL + g * 512: ec * TL + g * 512 + 512],
                                            in0=PSA[:, 0:512], scalar1=swinv[:, sw:sw + 1],
                                            scalar2=None, op0=ALU.mult)
                            v.drain()
                            v.engine_nop().then_inc(sDV, 1)
                        C.inc("DV", 1)
                for tg in range(4):
                    i = g * 4 + tg
                    Vt = V_tiles[i % 2]
                    @block.tensor
                    def _(t, tg=tg):
                        t.wait_ge(sDV, C.get("DV"))
                        for j in range(DC):
                            lhsT = uT_stage[:, j * 512 + tg * 128: j * 512 + (tg + 1) * 128]
                            mm0 = t.matmul(PSB[:, 0:512], lhsT,
                                           WT["v"][:, j * D:j * D + 512],
                                           start=(j == 0), stop=(j == DC - 1))
                            mm1 = t.matmul(PSB[:, 512:1024], lhsT,
                                           WT["v"][:, j * D + 512:(j + 1) * D],
                                           start=(j == 0), stop=(j == DC - 1))
                        mm1.then_inc(sPE, 1)
                    C.inc("PE", 1)
                    @block.vector
                    def _(v, Vt=Vt):
                        v.wait_ge(sPE, C.get("PE"))
                        v.wait_ge(sGP, C.get("GP"))
                        v.memset(Vt.rearrange("p (h c) -> p h c", h=H)[:, :, 64:65], 1.0)
                        v.drain()
                        v.tensor_scalar(out=Vt.rearrange("p (h c) -> p h c", h=H)[:, :, 0:64],
                                        in0=PSB[:, 0:1024].rearrange("p (h c) -> p h c", h=H),
                                        scalar1=swinv[:, 2:3], scalar2=None,
                                        op0=ALU.mult)
                        v.drain()
                        v.engine_nop().then_inc(sDV, 1)
                    C.inc("DV", 1)
                    @block.gpsimd
                    def _(g_, i=i, Vt=Vt):
                        g_.wait_ge(sDV, C.get("DV"))
                        g_.dma_start(out=agv_in.ap().rearrange("(t p) c -> p t c", p=128)[:, i, :],
                                     in_=Vt).then_inc(sGP, 16)
                    C.inc("GP", 16)
            if globals().get("DBG_E", 0) == 1:
                dbgE = nc.dram_tensor("dbgE", [128, 6160], F32, kind="ExternalOutput")
                @block.vector
                def _(v):
                    v.wait_ge(sPE, C.get("PE"))
                    v.wait_ge(sDV, C.get("DV"))
                    v.tensor_copy(V(A2, 49152, F32, 2048), qTsb[:, 0:2048])
                    v.tensor_copy(V(A2, 57344, F32, 2048), kTsb[:, 0:2048]).then_inc(sDV, 1)
                C.inc("DV", 1)
                @block.sync
                def _(s):
                    s.wait_ge(sDV, C.get("DV"))
                    s.dma_start(out=dbgE.ap()[:, 0:2048], in_=V(A2, 49152, F32, 2048)).then_inc(sST, 16)
                    s.dma_start(out=dbgE.ap()[:, 2048:4096], in_=V(A2, 57344, F32, 2048)).then_inc(sST, 16)
                    s.dma_start(out=dbgE.ap()[:, 4096:4100], in_=swrec[:, :]).then_inc(sST, 16)
                    s.dma_start(out=dbgE.ap()[:, 4100:4620].bitcast(BF16), in_=V(A6, 0, BF16, H * 65)).then_inc(sST, 16)
                    s.dma_start(out=dbgE.ap()[:, 5000:6024].bitcast(BF16)[:, 0:2048], in_=uT_stage[:, 0:2048]).then_inc(sST, 16)
                C.inc("ST", 16 * 5)

            @block.gpsimd
            def _(g_):
                g_.wait_ge(sDV, C.get("DV"))
                g_.dma_start(out=agk_in.ap().rearrange("(ec p) t -> p ec t", p=128),
                             in_=kTsb.rearrange("p (ec t) -> p ec t", ec=DC)).then_inc(sGP, 16)
                g_.wait_ge(sGP, C.inc("GP", 16))
                g_.collective_compute(
                    "AllGather", ALU.bypass, replica_groups=[list(range(NCORE))],
                    ins=[agk_in.ap().opt()], outs=[agk_out.ap().opt()]).then_inc(sCC, 1)
                g_.collective_compute(
                    "AllGather", ALU.bypass, replica_groups=[list(range(NCORE))],
                    ins=[agv_in.ap().opt()], outs=[agv_out.ap().opt()]).then_inc(sCC, 1)
            C.inc("CC", 2)
            cc_done = C.get("CC")
            @block.sync
            def _(s):
                s.dma_start(out=spill_x.ap(), in_=xres).then_inc(sST, 16)
            C.inc("ST", 16)
            spill_done = C.get("ST")
            KT_win = V(A5, 0, BF16, DC * TL)
            V_win = V(A1, 0, BF16, 16 * H * 65)
            ctx_acc = V(A2, 0, F32, NT * D)
            A_buf = [V(A6, 0, BF16, 2048), V(A6, 4096, BF16, 2048)]
            normbuf = V(A6, 8192, F32, 1024)
            @block.vector
            def _(v):
                v.wait_ge(sST, spill_done)
                v.wait_ge(sGP, C.get("GP"))
                v.memset(ctx_acc, 0.0)
                v.memset(normbuf, 0.0).then_inc(sDV, 1)
            C.inc("DV", 1)
            @block.gpsimd
            def _(g_):
                g_.wait_ge(sDV, C.get("DV"))
                g_.dma_start(out=norm_dram.ap()[:, 0:1024], in_=normbuf[0:16, :]).then_inc(sGP, 16)
                g_.dma_start(out=norm_dram.ap()[:, 1024:2048], in_=normbuf[16:32, :]).then_inc(sGP, 16)
            C.inc("GP", 32)
            for r in range(globals().get("DBG_R", NCORE)):
                @block.sync
                def _(s, r=r):
                    if r == 0:
                        s.wait_ge(sCC, cc_done)
                        s.wait_ge(sDV, C.get("DV"))
                    else:
                        s.wait_ge(sPE, C.get("PE"))
                    s.dma_start(out=KT_win.rearrange("p (ec t) -> p ec t", ec=DC),
                                in_=agk_out.ap()[r * D:(r + 1) * D, :]
                                .rearrange("(ec p) t -> p ec t", p=128)).then_inc(sLD, 16)
                    s.dma_start(out=V_win.rearrange("p (tc c) -> p tc c", tc=16),
                                in_=agv_out.ap()[r * TL:(r + 1) * TL, :]
                                .rearrange("(tc p) c -> p tc c", p=128)).then_inc(sLD, 16)
                C.inc("LD", 32)
                ld_win = C.get("LD")
                for h in range(globals().get("DBG_H", H)):
                    hp, rs = h // 2, (h % 2) * 64
                    ac_base = C.get("AC")
                    pe_base = C.get("PE")
                    NTC = globals().get("DBG_TC", 16)
                    for tc in range(NTC):
                        @block.tensor
                        def _(t, h=h, hp=hp, rs=rs, tc=tc, ld_win=ld_win):
                            if tc == 0:
                                t.wait_ge(sLD, ld_win)
                                t.wait_ge(sDV, C.get("DV"))
                            for qb in range(4):
                                mm = t.matmul(PSA[:, qb * 512:(qb + 1) * 512],
                                              KT_win[rs:rs + 64, hp * TL + tc * 128: hp * TL + (tc + 1) * 128],
                                              qTsb[rs:rs + 64, hp * TL + qb * 512: hp * TL + qb * 512 + 512],
                                              start=True, stop=True)
                            mm.then_inc(sPE, 1)
                        C.inc("PE", 1)
                        @block.scalar
                        def _(sc, tc=tc):
                            sc.wait_ge(sPE, C.get("PE"))
                            sc.activation(A_buf[tc % 2], PSA[:, 0:2048], AF.Exp,
                                          scale=0.125).then_inc(sAC, 1)
                        C.inc("AC", 1)
                        @block.tensor
                        def _(t, h=h, tc=tc):
                            t.wait_ge(sAC, C.get("AC"))
                            if tc > 0:
                                t.wait_ge(sDV, C.get("DV"))
                            for qb in range(4):
                                mm = t.matmul(PSB[0:65, qb * 512:(qb + 1) * 512],
                                              V_win[:, tc * H * 65 + h * 65: tc * H * 65 + (h + 1) * 65],
                                              A_buf[tc % 2][:, qb * 512:(qb + 1) * 512],
                                              start=True, stop=True)
                            mm.then_inc(sPE, 1)
                        C.inc("PE", 1)
                        @block.vector
                        def _(v, h=h, hp=hp, rs=rs, tc=tc):
                            v.wait_ge(sPE, C.get("PE"))
                            if tc == 0:
                                v.wait_ge(sGP, C.get("GP"))
                            for qb in range(4):
                                dst = ctx_acc[rs:rs + 64, hp * TL + qb * 512: hp * TL + qb * 512 + 512]
                                v.tensor_tensor(out=dst, in0=dst, in1=PSB[0:64, qb * 512:(qb + 1) * 512],
                                                op=ALU.add)
                            if tc == 0:
                                v.tensor_copy(normbuf[64:65, :], PSB[64:65, 0:1024])
                                ni = v.tensor_copy(normbuf[96:97, :], PSB[64:65, 1024:2048])
                            else:
                                v.tensor_tensor(out=normbuf[64:65, :], in0=normbuf[64:65, :],
                                                in1=PSB[64:65, 0:1024], op=ALU.add)
                                ni = v.tensor_tensor(out=normbuf[96:97, :], in0=normbuf[96:97, :],
                                                     in1=PSB[64:65, 1024:2048], op=ALU.add)
                            if tc == NTC - 1:
                                v.drain()
                                v.engine_nop().then_inc(sDV, 1)
                            else:
                                ni.then_inc(sDV, 1)
                        C.inc("DV", 1)
                    @block.gpsimd
                    def _(g_, h=h):
                        g_.wait_ge(sDV, C.get("DV"))
                        g_.dma_start(out=norm_dram.ap()[h:h + 1, 0:1024],
                                     in_=normbuf[64:65, :], accum_op=ALU.add).then_inc(sGP, 16)
                        g_.dma_start(out=norm_dram.ap()[h:h + 1, 1024:2048],
                                     in_=normbuf[96:97, :], accum_op=ALU.add).then_inc(sGP, 16)
                    C.inc("GP", 32)
            if globals().get("DBG_E", 0) == 2:
                dbgE2 = nc.dram_tensor("dbgE2", [128, 10240], F32, kind="ExternalOutput")
                @block.vector
                def _(v):
                    v.wait_ge(sPE, C.get("PE"))
                    v.wait_ge(sDV, C.get("DV"))
                    v.tensor_copy(V(A2, 65536 - 16384, F32, 2048), PSA[:, 0:2048])
                    v.tensor_copy(V(A2, 65536 - 8192, F32, 2048), PSB[:, 0:2048]).then_inc(sDV, 1)
                C.inc("DV", 1)
                @block.sync
                def _(s):
                    s.wait_ge(sDV, C.get("DV"))
                    s.wait_ge(sGP, C.get("GP"))
                    s.dma_start(out=dbgE2.ap()[:, 0:2048], in_=ctx_acc[:, 0:2048]).then_inc(sST, 16)
                    s.dma_start(out=dbgE2.ap()[0:16, 2048:4096], in_=norm_dram.ap()[:, :]).then_inc(sST, 16)
                    s.dma_start(out=dbgE2.ap()[:, 4096:4608].bitcast(BF16)[:, 0:1024],
                                in_=KT_win[:, 512:1536]).then_inc(sST, 16)
                    s.dma_start(out=dbgE2.ap()[:, 4608:5120].bitcast(BF16)[:, 0:1024],
                                in_=A_buf[0][:, 0:1024]).then_inc(sST, 16)
                    s.dma_start(out=dbgE2.ap()[:, 5120:7168], in_=V(A2, 65536 - 16384, F32, 2048)).then_inc(sST, 16)
                    s.dma_start(out=dbgE2.ap()[:, 7168:9216], in_=V(A2, 65536 - 8192, F32, 2048)).then_inc(sST, 16)
                    s.dma_start(out=dbgE2.ap()[:, 9216:10240].bitcast(BF16)[:, 0:2048],
                                in_=A_buf[1]).then_inc(sST, 16)
                C.inc("ST", 16 * 7)
                dbgE3 = nc.dram_tensor("dbgE3", [128, 16512], F32, kind="ExternalOutput")
                @block.sync
                def _(s):
                    s.dma_start(out=dbgE3.ap()[:, 0:8320].bitcast(BF16)[:, 0:16640],
                                in_=V_win[:, :]).then_inc(sST, 16)
                    s.dma_start(out=dbgE3.ap()[:, 8320:16512].bitcast(BF16)[:, 0:16384],
                                in_=qTsb[:, :]).then_inc(sST, 16)
                C.inc("ST", 32)

            rn_cur = V(A4, 0, F32, 2048)
            rn_rec = V(A4, 8192, F32, 2048)
            ctx_tok = V(A1, 0, F32, NT * D)
            tmpn = V(A6, 8192, F32, 512)
            for h in range(H):
                hp, rs = h // 2, (h % 2) * 64
                @block.sync
                def _(s, h=h):
                    s.wait_ge(sDV, C.get("DV"))
                    s.wait_ge(sGP, C.get("GP"))
                    s.wait_ge(sPE, C.get("PE"))
                    s.dma_start(out=rn_cur[0:1, :], in_=norm_dram.ap()[h:h + 1, :]).then_inc(sLD, 16)
                C.inc("LD", 16)
                @block.vector
                def _(v):
                    v.wait_ge(sLD, C.get("LD"))
                    v.reciprocal(rn_rec[0:1, :], rn_cur[0:1, :])
                    v.drain()
                    v.engine_nop().then_inc(sDV, 1)
                C.inc("DV", 1)
                for qb in range(4):
                    @block.tensor
                    def _(t, h=h, qb=qb):
                        t.wait_ge(sDV, C.get("DV"))
                        t.matmul(PSA[0:64, 0:512], ones128_t[0:1, 0:64],
                                 rn_rec[0:1, qb * 512:(qb + 1) * 512],
                                 start=True, stop=True).then_inc(sPE, 1)
                    C.inc("PE", 1)
                    @block.vector
                    def _(v, hp=hp, rs=rs, qb=qb):
                        v.wait_ge(sPE, C.get("PE"))
                        v.tensor_tensor(out=tmpn[0:64, :],
                                        in0=ctx_acc[rs:rs + 64, hp * TL + qb * 512: hp * TL + qb * 512 + 512],
                                        in1=PSA[0:64, 0:512], op=ALU.mult).then_inc(sDV, 1)
                    C.inc("DV", 1)
                    @block.tensor
                    def _(t):
                        t.wait_ge(sDV, C.get("DV"))
                        for sub in range(4):
                            mm = t.transpose(PSA[:, 1024 + sub * 64:1024 + (sub + 1) * 64],
                                             tmpn[0:64, sub * 128:(sub + 1) * 128], ident32_t[0:64, 0:64])
                        mm.then_inc(sPE, 1)
                    C.inc("PE", 1)
                    @block.vector
                    def _(v, h=h, qb=qb):
                        v.wait_ge(sPE, C.get("PE"))
                        ct3 = ctx_tok.rearrange("p (t d) -> p t d", t=NT)
                        dst = ct3[:, qb * 4:(qb + 1) * 4, h * 64:(h + 1) * 64]
                        v.tensor_copy(dst, PSA[:, 1024:1280].rearrange("p (s d) -> p s d", s=4)).then_inc(sDV, 1)
                    C.inc("DV", 1)
            @block.sync
            def _(s):
                s.wait_ge(sDV, C.get("DV"))
                s.dma_start(out=V(A2, 0, F32, NT * D), in_=spill_x.ap()).then_inc(sLD, 16)
                s.dma_start(out=V(A5, 0, BF16, DC * D), in_=tern_sp["o"].ap()).then_inc(sLD, 16)
            C.inc("LD", 32)
            ld_back = C.get("LD")
            xres2 = V(A2, 0, F32, NT * D)
            woT = V(A5, 0, BF16, DC * D)
            ctxqT = V(A6, 6144, BF16, DC * 512)
            for g in range(NG):
                for tg in range(4):
                    i = g * 4 + tg
                    ct = ctx_tok[:, i * D:(i + 1) * D]
                    @block.vector
                    def _(v, ct=ct):
                        v.wait_ge(sPE, C.get("PE"))
                        v.tensor_tensor(out=V(A6, 2048, F32, 512), in0=ct[:, 0:512],
                                        in1=ct[:, 0:512], op=ALU.mult)
                        v.tensor_reduce(ln_sc[:, 11:12], V(A6, 2048, F32, 512),
                                        mybir.AxisListType.X, ALU.add)
                        v.tensor_tensor(out=V(A6, 2048, F32, 512), in0=ct[:, 512:1024],
                                        in1=ct[:, 512:1024], op=ALU.mult)
                        v.tensor_reduce(ln_sc[:, 14:15], V(A6, 2048, F32, 512),
                                        mybir.AxisListType.X, ALU.add)
                        v.drain()
                        v.tensor_tensor(out=ln_sc[:, 11:12], in0=ln_sc[:, 11:12],
                                        in1=ln_sc[:, 14:15], op=ALU.add)
                        v.drain()
                        v.tensor_scalar(out=ln_sc[:, 11:12], in0=ln_sc[:, 11:12],
                                        scalar1=1.0 / D, scalar2=RMS_EPS,
                                        op0=ALU.mult, op1=ALU.add)
                        v.drain()
                        v.reciprocal(ln_sc[:, 12:13], ln_sc[:, 11:12])
                        v.drain()
                        v.engine_nop().then_inc(sDV, 1)
                    C.inc("DV", 1)
                    @block.scalar
                    def _(sc):
                        sc.wait_ge(sDV, C.get("DV"))
                        sc.activation(ln_sc[:, 12:13], ln_sc[:, 12:13], AF.Sqrt).then_inc(sAC, 1)
                    C.inc("AC", 1)
                    @block.vector
                    def _(v, ct=ct):
                        v.wait_ge(sAC, C.get("AC"))
                        emit_rsqrt_refine(v, ln_sc[:, 12:13], ln_sc[:, 11:12],
                                          ln_sc[:, 9:10], ln_sc[:, 10:11])
                        v.reduce_max(ln_sc[:, 5:6], ct, axis=mybir.AxisListType.X,
                                     apply_absolute_value=True)
                        v.drain()
                        v.tensor_tensor(out=ln_sc[:, 5:6], in0=ln_sc[:, 5:6],
                                        in1=ln_sc[:, 12:13], op=ALU.mult)
                        v.drain()
                        v.tensor_scalar(out=ln_sc[:, 6:7], in0=ln_sc[:, 5:6],
                                        scalar1=AQ_EPS, scalar2=1.0 / 127.0,
                                        op0=ALU.max, op1=ALU.mult)
                        v.drain()
                        v.reciprocal(ln_sc[:, 7:8], ln_sc[:, 6:7])
                        v.drain()
                        v.tensor_tensor(out=ln_sc[:, 13:14], in0=ln_sc[:, 12:13],
                                        in1=ln_sc[:, 7:8], op=ALU.mult)
                        v.drain()
                        PSB_i2 = PSB.ap().bitcast(I32)
                        for hh in range(2):
                            v.tensor_scalar(out=PSB_i2[:, 1536:2048], in0=ct[:, hh * 512:(hh + 1) * 512],
                                            scalar1=ln_sc[:, 13:14], scalar2=None, op0=ALU.mult)
                            v.drain()
                            v.tensor_scalar(out=ub_bf[:, hh * 512:(hh + 1) * 512], in0=PSB_i2[:, 1536:2048],
                                            scalar1=ln_sc[:, 6:7], scalar2=None, op0=ALU.mult)
                            v.drain()
                        v.engine_nop().then_inc(sDV, 1)
                    C.inc("DV", 1)
                    @block.tensor
                    def _(t):
                        t.wait_ge(sDV, C.get("DV"))
                        for j in range(DC):
                            mm = t.transpose(PSB_bf[:, 2048 + j * 128:2048 + (j + 1) * 128],
                                             ub_bf[:, j * 128:(j + 1) * 128], identb_t[:, :])
                        mm.then_inc(sPE, 1)
                    C.inc("PE", 1)
                    @block.vector
                    def _(v, tg=tg):
                        v.wait_ge(sPE, C.get("PE"))
                        dst = ctxqT.rearrange("p (j t) -> p j t", j=DC)[:, :, tg * 128:(tg + 1) * 128]
                        v.tensor_copy(dst, PSB_bf[:, 2048:3072].rearrange("p (j t) -> p j t", j=DC)).then_inc(sDV, 1)
                    C.inc("DV", 1)
                for oc in range(DC):
                    @block.tensor
                    def _(t, oc=oc):
                        t.wait_ge(sDV, C.get("DV"))
                        t.wait_ge(sLD, ld_back)
                        for j in range(DC):
                            mm = t.matmul(PSA[:, 0:512],
                                          woT[:, j * D + oc * 128:j * D + (oc + 1) * 128],
                                          ctxqT[:, j * 512:(j + 1) * 512],
                                          start=(j == 0), stop=(j == DC - 1))
                        mm.then_inc(sPE, 1)
                    C.inc("PE", 1)
                    @block.vector
                    def _(v, oc=oc):
                        v.wait_ge(sPE, C.get("PE"))
                        v.tensor_scalar(out=V(A6, 0, F32, 512), in0=PSA[:, 0:512],
                                        scalar1=swinv[:, 3:4], scalar2=None,
                                        op0=ALU.mult).then_inc(sDV, 1)
                    C.inc("DV", 1)
                    @block.tensor
                    def _(t):
                        t.wait_ge(sDV, C.get("DV"))
                        for sub in range(4):
                            mm = t.transpose(PSB[:, 512 + sub * 128:512 + (sub + 1) * 128],
                                             V(A6, 0, F32, 512)[:, sub * 128:(sub + 1) * 128],
                                             ident32_t[:, :])
                        mm.then_inc(sPE, 1)
                    C.inc("PE", 1)
                    @block.vector
                    def _(v, g=g, oc=oc):
                        v.wait_ge(sPE, C.get("PE"))
                        xr3 = xres2.rearrange("p (t d) -> p t d", t=NT)
                        dst = xr3[:, g * 4:(g + 1) * 4, oc * 128:(oc + 1) * 128]
                        v.tensor_tensor(out=dst, in0=dst,
                                        in1=PSB[:, 512:1024].rearrange("p (s d) -> p s d", s=4),
                                        op=ALU.add).then_inc(sDV, 1)
                    C.inc("DV", 1)
            @block.vector
            def _(v):
                v.wait_ge(sDV, C.get("DV"))
                v.tensor_copy(xres, xres2).then_inc(sDV, 1)
            C.inc("DV", 1)


        def l1attn_phase_v2():
            WT = {nm: V(A2, mi * 16384, BF16, DC * D)
                  for mi, nm in enumerate(("q", "k", "v", "o"))}
            agx_in = nc.dram_tensor("agx_in", [TL, D], BF16)
            agx_out = nc.dram_tensor("agx_out", [T, D], BF16)
            sel_in = nc.dram_tensor("sel", [32], F32, kind="ExternalInput")
            for mi, nm in enumerate(("q", "k", "v", "o")):
                @block.sync
                def _(s, mi=mi, nm=nm):
                    s.wait_ge(sDV, C.get("DV"))
                    s.wait_ge(sPE, C.get("PE"))
                    s.dma_start(out=V(A2, mi * 16384, BF16, DC * D),
                                in_=tern_sp[nm].ap()).then_inc(sLD, 16)
            C.inc("LD", 16 * 4)
            sel_row = V(A4, 11008, F32, 32)
            selb = V(A6, 14336, F32, 32)
            @block.sync
            def _(s):
                s.dma_start(out=sel_row[0:1, :], in_=sel_in.ap().unsqueeze(0)).then_inc(sLD, 16)
            C.inc("LD", 16)
            ld_w = C.get("LD")
            @block.tensor
            def _(t):
                t.wait_ge(sLD, ld_w)
                t.wait_ge(sDV, C.get("DV"))
                t.matmul(PSB[:, 0:32], ones128_t[0:1, :], sel_row[0:1, :],
                         start=True, stop=True).then_inc(sPE, 1)
            C.inc("PE", 1)
            @block.vector
            def _(v):
                v.wait_ge(sPE, C.get("PE"))
                v.tensor_copy(selb, PSB[:, 0:32])
                v.drain()
                v.engine_nop().then_inc(sDV, 1)
            C.inc("DV", 1)
            q_sb = V(A4, 0, BF16, D)
            k_sb = V(A4, 2048, BF16, D)
            v_sb = V(A4, 4096, BF16, D)
            ctxb = V(A4, 6144, BF16, D)
            scores = V(A4, 8192, F32, 256)
            e_sb = V(A4, 9216, F32, 256)
            pbuf = [V(A4, 10240, F32, 64), V(A4, 10752, F32, 64)]
            smx = V(A4, 10496, F32, 16)
            negm = V(A4, 10560, F32, 16)
            den = V(A4, 10624, F32, 16)
            rden = V(A4, 10688, F32, 16)
            ctxf = V(A4, 12288, F32, D)

            def emit_rms_actquant():
                @block.vector
                def _(v):
                    v.wait_ge(sPE, C.get("PE"))
                    PSB_f = PSB.ap()
                    v.tensor_tensor(out=PSB_f[:, 1024:1536], in0=u_f32[:, 0:512],
                                    in1=u_f32[:, 0:512], op=ALU.mult)
                    v.tensor_reduce(ln_sc[:, 11:12], PSB_f[:, 1024:1536],
                                    mybir.AxisListType.X, ALU.add)
                    v.tensor_tensor(out=PSB_f[:, 1024:1536], in0=u_f32[:, 512:1024],
                                    in1=u_f32[:, 512:1024], op=ALU.mult)
                    v.tensor_reduce(ln_sc[:, 14:15], PSB_f[:, 1024:1536],
                                    mybir.AxisListType.X, ALU.add)
                    v.drain()
                    v.tensor_tensor(out=ln_sc[:, 11:12], in0=ln_sc[:, 11:12],
                                    in1=ln_sc[:, 14:15], op=ALU.add)
                    v.drain()
                    v.tensor_scalar(out=ln_sc[:, 11:12], in0=ln_sc[:, 11:12],
                                    scalar1=1.0 / D, scalar2=RMS_EPS,
                                    op0=ALU.mult, op1=ALU.add)
                    v.drain()
                    v.reciprocal(ln_sc[:, 12:13], ln_sc[:, 11:12])
                    v.drain()
                    v.engine_nop().then_inc(sDV, 1)
                C.inc("DV", 1)
                @block.scalar
                def _(sc):
                    sc.wait_ge(sDV, C.get("DV"))
                    sc.activation(ln_sc[:, 12:13], ln_sc[:, 12:13], AF.Sqrt).then_inc(sAC, 1)
                C.inc("AC", 1)
                @block.vector
                def _(v):
                    v.wait_ge(sAC, C.get("AC"))
                    emit_rsqrt_refine(v, ln_sc[:, 12:13], ln_sc[:, 11:12],
                                      ln_sc[:, 9:10], ln_sc[:, 10:11])
                    v.reduce_max(ln_sc[:, 5:6], u_f32, axis=mybir.AxisListType.X,
                                 apply_absolute_value=True)
                    v.drain()
                    v.tensor_tensor(out=ln_sc[:, 5:6], in0=ln_sc[:, 5:6],
                                    in1=ln_sc[:, 12:13], op=ALU.mult)
                    v.drain()
                    v.tensor_scalar(out=ln_sc[:, 6:7], in0=ln_sc[:, 5:6],
                                    scalar1=AQ_EPS, scalar2=1.0 / 127.0,
                                    op0=ALU.max, op1=ALU.mult)
                    v.drain()
                    v.reciprocal(ln_sc[:, 7:8], ln_sc[:, 6:7])
                    v.drain()
                    v.tensor_tensor(out=ln_sc[:, 13:14], in0=ln_sc[:, 12:13],
                                    in1=ln_sc[:, 7:8], op=ALU.mult)
                    v.drain()
                    PSB_i = PSB.ap().bitcast(I32)
                    for hh in range(2):
                        v.tensor_scalar(out=PSB_i[:, 1536:2048], in0=u_f32[:, hh * 512:(hh + 1) * 512],
                                        scalar1=ln_sc[:, 13:14], scalar2=None, op0=ALU.mult)
                        v.drain()
                        v.tensor_scalar(out=ub_bf[:, hh * 512:(hh + 1) * 512], in0=PSB_i[:, 1536:2048],
                                        scalar1=ln_sc[:, 6:7], scalar2=None, op0=ALU.mult)
                        v.drain()
                    v.engine_nop().then_inc(sDV, 1)
                C.inc("DV", 1)

            for g in range(NG):
                for tg in range(4):
                    i = g * 4 + tg
                    emit_tile_ln(xres[:, i * D:(i + 1) * D])
                    emit_rms_actquant()
                    emit_transpose_to_stage(ub_bf, tg)
                for tg in range(4):
                    i = g * 4 + tg
                    @block.tensor
                    def _(t, tg=tg):
                        t.wait_ge(sDV, C.get("DV"))
                        for nm, off in (("q", 0), ("k", 1024)):
                            for j in range(DC):
                                lhsT = uT_stage[:, j * 512 + tg * 128: j * 512 + (tg + 1) * 128]
                                t.matmul(PSA[:, off:off + 512], lhsT,
                                         WT[nm][:, j * D:j * D + 512],
                                         start=(j == 0), stop=(j == DC - 1))
                                t.matmul(PSA[:, off + 512:off + 1024], lhsT,
                                         WT[nm][:, j * D + 512:(j + 1) * D],
                                         start=(j == 0), stop=(j == DC - 1))
                        for j in range(DC):
                            lhsT = uT_stage[:, j * 512 + tg * 128: j * 512 + (tg + 1) * 128]
                            t.matmul(PSB[:, 0:512], lhsT, WT["v"][:, j * D:j * D + 512],
                                     start=(j == 0), stop=(j == DC - 1))
                            mm = t.matmul(PSB[:, 512:1024], lhsT, WT["v"][:, j * D + 512:(j + 1) * D],
                                          start=(j == 0), stop=(j == DC - 1))
                        mm.then_inc(sPE, 1)
                    C.inc("PE", 1)
                    @block.vector
                    def _(v):
                        v.wait_ge(sPE, C.get("PE"))
                        v.wait_ge(sGP, C.get("GP"))
                        v.tensor_scalar(out=q_sb, in0=PSA[:, 0:1024], scalar1=swinv[:, 0:1],
                                        scalar2=None, op0=ALU.mult)
                        v.tensor_scalar(out=k_sb, in0=PSA[:, 1024:2048], scalar1=swinv[:, 1:2],
                                        scalar2=None, op0=ALU.mult)
                        v.tensor_scalar(out=v_sb, in0=PSB[:, 0:1024], scalar1=swinv[:, 2:3],
                                        scalar2=None, op0=ALU.mult)
                        v.drain()
                        pairs = [(h, tt) for h in range(H) for tt in range(H)]
                        for n, (h, tt) in enumerate(pairs):
                            v.tensor_tensor(out=pbuf[n % 2], in0=q_sb[:, h * 64:(h + 1) * 64],
                                            in1=k_sb[:, tt * 64:(tt + 1) * 64], op=ALU.mult)
                            if n > 0:
                                hp_, tp_ = pairs[n - 1]
                                v.tensor_reduce(scores[:, hp_ * H + tp_: hp_ * H + tp_ + 1],
                                                pbuf[(n - 1) % 2], mybir.AxisListType.X, ALU.add)
                        v.drain()
                        v.tensor_reduce(scores[:, 255:256], pbuf[255 % 2],
                                        mybir.AxisListType.X, ALU.add)
                        v.drain()
                        for h in range(H):
                            v.reduce_max(smx[:, h:h + 1], scores[:, h * H:(h + 1) * H],
                                         axis=mybir.AxisListType.X)
                        v.drain()
                        v.tensor_scalar(out=negm, in0=smx, scalar1=-0.125, scalar2=None,
                                        op0=ALU.mult)
                        v.drain()
                        v.engine_nop().then_inc(sDV, 1)
                    C.inc("DV", 1)
                    @block.scalar
                    def _(sc):
                        sc.wait_ge(sDV, C.get("DV"))
                        for h in range(H):
                            a = sc.activation(e_sb[:, h * H:(h + 1) * H], scores[:, h * H:(h + 1) * H],
                                              AF.Exp, bias=negm[:, h:h + 1], scale=0.125)
                        a.then_inc(sAC, 1)
                    C.inc("AC", 1)
                    @block.vector
                    def _(v):
                        v.wait_ge(sAC, C.get("AC"))
                        for h in range(H):
                            v.tensor_reduce(den[:, h:h + 1], e_sb[:, h * H:(h + 1) * H],
                                            mybir.AxisListType.X, ALU.add)
                        v.drain()
                        v.reciprocal(rden, den)
                        v.drain()
                        for h in range(H):
                            v.tensor_scalar(out=ctxf[:, h * 64:(h + 1) * 64], in0=v_sb[:, 0:64],
                                            scalar1=e_sb[:, h * H:h * H + 1], scalar2=None,
                                            op0=ALU.mult)
                            for tt in range(1, H):
                                v.scalar_tensor_tensor(out=ctxf[:, h * 64:(h + 1) * 64],
                                                       in0=v_sb[:, tt * 64:(tt + 1) * 64],
                                                       scalar=e_sb[:, h * H + tt:h * H + tt + 1],
                                                       in1=ctxf[:, h * 64:(h + 1) * 64],
                                                       op0=ALU.mult, op1=ALU.add)
                            v.tensor_scalar(out=ctxf[:, h * 64:(h + 1) * 64],
                                            in0=ctxf[:, h * 64:(h + 1) * 64],
                                            scalar1=rden[:, h:h + 1], scalar2=None, op0=ALU.mult)
                        v.drain()
                        v.tensor_copy(ctxb, ctxf)
                        v.drain()
                        v.engine_nop().then_inc(sDV, 1)
                    C.inc("DV", 1)
                    @block.gpsimd
                    def _(g_, i=i):
                        g_.wait_ge(sDV, C.get("DV"))
                        ag3 = agx_in.ap().rearrange("(hh sq) c -> hh sq c", hh=16)
                        for h in range(H):
                            g_.dma_start(
                                out=ag3[h:h + 1, i * 8:(i + 1) * 8, :]
                                .rearrange("one sq (r d) -> (one sq r) d", r=16),
                                in_=ctxb[:, h * 64:(h + 1) * 64]).then_inc(sGP, 16)
                    C.inc("GP", 16 * 16)
            @block.gpsimd
            def _(g_):
                g_.wait_ge(sGP, C.get("GP"))
                g_.collective_compute(
                    "AllGather", ALU.bypass, replica_groups=[list(range(NCORE))],
                    ins=[agx_in.ap().opt()], outs=[agx_out.ap().opt()]).then_inc(sCC, 1)
            C.inc("CC", 1)
            cc2 = C.get("CC")
            blockbuf = V(A5, 0, BF16, 16 * D)
            ctx2 = V(A4, 0, BF16, NT * D)
            pickf = V(A6, 0, F32, D)
            for cc in range(NCORE):
                @block.sync
                def _(s, cc=cc):
                    if cc == 0:
                        s.wait_ge(sCC, cc2)
                        s.wait_ge(sDV, C.get("DV"))
                    else:
                        s.wait_ge(sDV, C.get("DV"))
                    s.dma_start(out=blockbuf.rearrange("p (m c) -> p m c", m=16),
                                in_=agx_out.ap()[cc * TL:(cc + 1) * TL, :]
                                .rearrange("(m p) c -> p m c", p=128)).then_inc(sLD, 16)
                C.inc("LD", 16)
                @block.vector
                def _(v, cc=cc):
                    v.wait_ge(sLD, C.get("LD"))
                    for u in range(2):
                        tdst = u * 8 + cc
                        v.tensor_scalar(out=pickf, in0=blockbuf[:, 0:1024],
                                        scalar1=selb[:, u * 16:u * 16 + 1], scalar2=None,
                                        op0=ALU.mult)
                        for m in range(1, 16):
                            v.scalar_tensor_tensor(out=pickf, in0=blockbuf[:, m * 1024:(m + 1) * 1024],
                                                   scalar=selb[:, u * 16 + m:u * 16 + m + 1],
                                                   in1=pickf, op0=ALU.mult, op1=ALU.add)
                        v.drain()
                        v.tensor_copy(ctx2[:, tdst * 1024:(tdst + 1) * 1024], pickf)
                        v.drain()
                    v.engine_nop().then_inc(sDV, 1)
                C.inc("DV", 1)
            for g in range(NG):
                for tg in range(4):
                    i = g * 4 + tg
                    @block.vector
                    def _(v, i=i):
                        v.wait_ge(sPE, C.get("PE"))
                        v.tensor_copy(u_f32, ctx2[:, i * D:(i + 1) * D])
                        v.drain()
                        v.engine_nop().then_inc(sDV, 1)
                    C.inc("DV", 1)
                    emit_rms_actquant()
                    emit_transpose_to_stage(ub_bf, tg)
                for tg in range(4):
                    i = g * 4 + tg
                    @block.tensor
                    def _(t, tg=tg):
                        t.wait_ge(sDV, C.get("DV"))
                        for j in range(DC):
                            lhsT = uT_stage[:, j * 512 + tg * 128: j * 512 + (tg + 1) * 128]
                            t.matmul(PSB[:, 0:512], lhsT, WT["o"][:, j * D:j * D + 512],
                                     start=(j == 0), stop=(j == DC - 1))
                            mm = t.matmul(PSB[:, 512:1024], lhsT, WT["o"][:, j * D + 512:(j + 1) * D],
                                          start=(j == 0), stop=(j == DC - 1))
                        mm.then_inc(sPE, 1)
                    C.inc("PE", 1)
                    @block.vector
                    def _(v, i=i):
                        v.wait_ge(sPE, C.get("PE"))
                        v.tensor_scalar(out=pickf, in0=PSB[:, 0:1024], scalar1=swinv[:, 3:4],
                                        scalar2=None, op0=ALU.mult)
                        v.drain()
                        xt = xres[:, i * D:(i + 1) * D]
                        v.tensor_tensor(out=xt, in0=xt, in1=pickf, op=ALU.add)
                        v.drain()
                        v.engine_nop().then_inc(sDV, 1)
                    C.inc("DV", 1)

        @block.tensor
        def _(t):
            t.wait_ge(sDV, w8t_dv)
            t.wait_ge(sLD, n_init_ld)
        @block.vector
        def _(v):
            v.wait_ge(sDV, w8t_dv)
            v.wait_ge(sLD, n_init_ld)

        tern_prep()
        l0attn_wprep()

        if phases >= 1:
            l0ffn_phase("A")
        if phases >= 2:
            l0attn_phase()
        if phases >= 3:
            l0ffn_phase("C")
        if phases >= 4:
            l1ffn_phase()
        if phases >= 5:
            l1attn_phase_v2()
        if phases >= 6:
            l1ffn_phase()

        @block.sync
        def _(s):
            s.wait_ge(sDV, C.get("DV"))
            s.wait_ge(sPE, C.get("PE"))
            s.wait_ge(sAC, C.get("AC"))
            s.dma_start(out=y_out.ap().rearrange("(t p) d -> p t d", p=128),
                        in_=xres.rearrange("p (t d) -> p t d", t=NT)).then_inc(sST, 16)
            s.wait_ge(sST, C.inc("ST", 16))

    return nc


def kernel(L0_attn_query, L0_attn_Wq, L0_attn_bq, L0_attn_Wk, L0_attn_bk,
           L0_attn_Wv, L0_attn_bv, L0_attn_Wo, L0_attn_bo,
           L0_ffn_weight_q, L0_ffn_weight_scale, L0_ffn_bias,
           L1_attn_wq, L1_attn_wk, L1_attn_wv, L1_attn_wo,
           L1_ffn_w1, L1_ffn_b1, L1_ffn_w2, L1_ffn_b2):
    x = np.ascontiguousarray(np.asarray(L0_attn_query, dtype=np.float32).reshape(T, D))
    shards = [np.ascontiguousarray(x[c * TL:(c + 1) * TL]) for c in range(NCORE)]
    common = {
        "l0w8": np.ascontiguousarray(np.asarray(L0_ffn_weight_q, dtype=np.int8)),
        "l0ws": np.ascontiguousarray(np.asarray(L0_ffn_weight_scale, dtype=np.float32)),
        "l0fb": np.ascontiguousarray(np.asarray(L0_ffn_bias, dtype=np.float32)),
        "l0a_wq": np.ascontiguousarray(np.asarray(L0_attn_Wq, dtype=np.float32)),
        "l0a_bq": np.ascontiguousarray(np.asarray(L0_attn_bq, dtype=np.float32)),
        "l0a_wk": np.ascontiguousarray(np.asarray(L0_attn_Wk, dtype=np.float32)),
        "l0a_bk": np.ascontiguousarray(np.asarray(L0_attn_bk, dtype=np.float32)),
        "l0a_wv": np.ascontiguousarray(np.asarray(L0_attn_Wv, dtype=np.float32)),
        "l0a_bv": np.ascontiguousarray(np.asarray(L0_attn_bv, dtype=np.float32)),
        "l0a_wo": np.ascontiguousarray(np.asarray(L0_attn_Wo, dtype=np.float32)),
        "l0a_bo": np.ascontiguousarray(np.asarray(L0_attn_bo, dtype=np.float32)),
        "l1a_wq": np.ascontiguousarray(np.asarray(L1_attn_wq, dtype=np.float32)),
        "l1a_wk": np.ascontiguousarray(np.asarray(L1_attn_wk, dtype=np.float32)),
        "l1a_wv": np.ascontiguousarray(np.asarray(L1_attn_wv, dtype=np.float32)),
        "l1a_wo": np.ascontiguousarray(np.asarray(L1_attn_wo, dtype=np.float32)),
        "w1": np.ascontiguousarray(np.asarray(L1_ffn_w1, dtype=np.float32)),
        "b1": np.ascontiguousarray(np.asarray(L1_ffn_b1, dtype=np.float32)),
        "w2": np.ascontiguousarray(np.asarray(L1_ffn_w2, dtype=np.float32)),
        "b2": np.ascontiguousarray(np.asarray(L1_ffn_b2, dtype=np.float32)),
    }
    nc = _build(PHASES)
    sels = []
    for c in range(NCORE):
        sel = np.zeros(32, dtype=np.float32)
        for u in range(2):
            sel[u * 16 + 2 * c + u] = 1.0
        sels.append(sel)
    in_maps = [dict(common, x=shards[c], **({"sel": sels[c]} if PHASES >= 5 else {}))
               for c in range(NCORE)]
    res = run_bass_kernel_spmd(nc, in_maps, core_ids=list(range(NCORE)))
    out = np.concatenate([np.asarray(r["y"], dtype=np.float32) for r in res.results], axis=0)
    return out.reshape(1, T, D)


# version 04 (final)
# speedup vs baseline: 1.0006x; 1.0006x over previous
import numpy as np
import ml_dtypes
import concourse.bass as bass
import concourse.mybir as mybir
from concourse.bass_utils import run_bass_kernel_spmd

F32 = mybir.dt.float32
BF16 = mybir.dt.bfloat16
I32 = mybir.dt.int32
I8 = mybir.dt.int8
AF = mybir.ActivationFunctionType
ALU = mybir.AluOpType

NCORE = 8
T = 16384
TL = T // NCORE
NT = TL // 128
NG = NT // 4
D = 1024
DC = D // 128
H = 16
DH = 64
FF = 4096
FC = FF // 128

LN_EPS = 1e-5
L0FFN_EPS = 1e-6
L0ATTN_EPS = 1e-6
RMS_EPS = 1e-6
AQ_EPS = 1e-5
WQ_EPS = 1e-5

PHASES = 6
DBG_STEP = 'full'


def _ceil32(x):
    return (x + 31) & ~31


class Ctr:

    def __init__(self):
        self.v = {}

    def inc(self, k, n=1):
        self.v[k] = self.v.get(k, 0) + n
        return self.v[k]

    def get(self, k):
        return self.v.get(k, 0)


def _build(phases=PHASES, detect_races=False):
    nc = bass.Bass("TRN2", target_bir_lowering=False,
                   detect_race_conditions=detect_races)

    x_in = nc.dram_tensor("x", [TL, D], F32, kind="ExternalInput")
    y_out = nc.dram_tensor("y", [TL, D], F32, kind="ExternalOutput")

    l0w8 = nc.dram_tensor("l0w8", [D, D], I8, kind="ExternalInput")
    l0ws = nc.dram_tensor("l0ws", [D], F32, kind="ExternalInput")
    l0fb = nc.dram_tensor("l0fb", [D], F32, kind="ExternalInput")
    aw = {}
    ab = {}
    for nm in ("q", "k", "v", "o"):
        aw[nm] = nc.dram_tensor(f"l0a_w{nm}", [D, D], F32, kind="ExternalInput")
        ab[nm] = nc.dram_tensor(f"l0a_b{nm}", [D], F32, kind="ExternalInput")
    bw = {}
    for nm in ("q", "k", "v", "o"):
        bw[nm] = nc.dram_tensor(f"l1a_w{nm}", [D, D], F32, kind="ExternalInput")
    w1 = nc.dram_tensor("w1", [D, FF], F32, kind="ExternalInput")
    b1 = nc.dram_tensor("b1", [FF], F32, kind="ExternalInput")
    w2 = nc.dram_tensor("w2", [FF, D], F32, kind="ExternalInput")
    b2 = nc.dram_tensor("b2", [D], F32, kind="ExternalInput")

    spill_x = nc.dram_tensor("spill_x", [128, NT * D], F32)
    tern_sp = {nm: nc.dram_tensor(f"tern_{nm}", [128, DC * D], BF16)
               for nm in ("q", "k", "v", "o")}
    agk_in = nc.dram_tensor("agk_in", [D, TL], BF16)
    agk_out = nc.dram_tensor("agk_out", [NCORE * D, TL], BF16)
    agv_in = nc.dram_tensor("agv_in", [TL, H * 65], BF16)
    agv_out = nc.dram_tensor("agv_out", [T, H * 65], BF16)
    kv_in = nc.dram_tensor("kv_in", [128, 528], F32)
    kv_out = nc.dram_tensor("kv_out", [128, 528], F32)
    norm_dram = nc.dram_tensor("norm_dram", [H, TL], F32)
    dbg_out = nc.dram_tensor("dbg", [128, 3072], F32, kind="ExternalOutput")

    identb = nc.inline_tensor(np.eye(128, dtype=ml_dtypes.bfloat16), name="identb")
    ident32 = nc.inline_tensor(np.eye(128, dtype=np.float32), name="ident32")
    ones128 = nc.inline_tensor(np.ones((128, 128), dtype=np.float32), name="ones128")
    onescol_b = nc.inline_tensor(np.ones((128, 1), dtype=ml_dtypes.bfloat16), name="onescol_b")

    from contextlib import ExitStack
    with ExitStack() as stk:
        E = stk.enter_context
        A1 = E(nc.sbuf_tensor("A1", [128, 65536], mybir.dt.uint8))
        A2 = E(nc.sbuf_tensor("A2", [128, 65536], mybir.dt.uint8))
        A4 = E(nc.sbuf_tensor("A4", [128, 32768], mybir.dt.uint8))
        A5 = E(nc.sbuf_tensor("A5", [128, 32768], mybir.dt.uint8))
        A6 = E(nc.sbuf_tensor("A6", [128, 14464], mybir.dt.uint8))

        def V(arena, off_bytes, dtype, cols):
            ap = arena.ap()[:, off_bytes:off_bytes + cols * mybir.dt.size(dtype)]
            return ap.bitcast(dtype)

        identb_t = E(nc.sbuf_tensor("identb_t", [128, 128], BF16))
        ident32_t = E(nc.sbuf_tensor("ident32_t", [128, 128], F32))
        ones128_t = E(nc.sbuf_tensor("ones128_t", [128, 128], F32))
        onescol_t = E(nc.sbuf_tensor("onescol_t", [128, 1], BF16))

        ln_sc = E(nc.sbuf_tensor("ln_sc", [128, 16], F32))
        swrec = E(nc.sbuf_tensor("swrec", [128, 4], F32))
        swinv = E(nc.sbuf_tensor("swinv", [128, 4], F32))
        bcol = E(nc.sbuf_tensor("bcol", [128, 8 * 4 + 32 + 8 + 1], F32))
        bq_col = bcol[:, 0:8]
        bk_col = bcol[:, 8:16]
        bv_col = bcol[:, 16:24]
        bo_col = bcol[:, 24:32]
        b1c = bcol[:, 32:64]
        b2c = bcol[:, 64:72]
        quarter_col = bcol[:, 72:73]


        PSA = E(nc.psum_tensor("PSA", [128, 2048], F32))
        PSB = E(nc.psum_tensor("PSB", [128, 2048], F32))
        PSA_bf = PSA.ap().bitcast(BF16)
        PSB_bf = PSB.ap().bitcast(BF16)

        sLD = E(nc.semaphore("sLD"))
        sGP = E(nc.semaphore("sGP"))
        sPE = E(nc.semaphore("sPE"))
        sDV = E(nc.semaphore("sDV"))
        sAC = E(nc.semaphore("sAC"))
        sCC = E(nc.semaphore("sCC"))
        sST = E(nc.semaphore("sST"))
        block = E(nc.Block())
        C = Ctr()


        @block.sync
        def _(s):
            s.dma_start(out=identb_t[:, :], in_=identb[:, :]).then_inc(sLD, 16)
            s.dma_start(out=ident32_t[:, :], in_=ident32[:, :]).then_inc(sLD, 16)
            s.dma_start(out=ones128_t[:, :], in_=ones128[:, :]).then_inc(sLD, 16)
            s.dma_start(out=onescol_t[:, :], in_=onescol_b[:, :]).then_inc(sLD, 16)
            s.dma_start(out=V(A1, 0, F32, NT * D).rearrange("p (t d) -> p t d", t=NT),
                        in_=x_in.ap().rearrange("(t p) d -> p t d", p=128)).then_inc(sLD, 16)
            with nc.allow_non_contiguous_dma(reason="tiny bias col loads"):
                for i, nm in enumerate(("q", "k", "v", "o")):
                    s.dma_start(out=bcol[:, i * 8:(i + 1) * 8],
                                in_=ab[nm].ap().rearrange("(c p) -> p c", p=128),
                                allow_slow_non_contiguous=True).then_inc(sLD, 16)
                s.dma_start(out=b1c, in_=b1.ap().rearrange("(c p) -> p c", p=128),
                            allow_slow_non_contiguous=True).then_inc(sLD, 16)
                s.dma_start(out=b2c, in_=b2.ap().rearrange("(c p) -> p c", p=128),
                            allow_slow_non_contiguous=True).then_inc(sLD, 16)
            s.dma_start(out=V(A6, 0, F32, D)[0:1, :],
                        in_=l0ws.ap().unsqueeze(0)).then_inc(sLD, 16)
            s.dma_start(out=V(A6, 4096, F32, D)[0:1, :],
                        in_=l0fb.ap().unsqueeze(0)).then_inc(sLD, 16)
        C.inc("LD", 16 * 13)
        n_init_ld = C.get("LD")

        w8stage = V(A2, 0, BF16, DC * D)
        W8T = V(A5, 0, BF16, DC * D)
        WS_bc = V(A5, 16384, F32, D)
        B0H_bc = V(A5, 20480, F32, D)
        @block.gpsimd
        def _(g):
            g.dma_start(out=w8stage.rearrange("p (c d) -> p c d", c=DC),
                        in_=l0w8.ap().rearrange("(c p) d -> p c d", p=128)).then_inc(sGP, 16)
        C.inc("GP", 16)
        w8_gp = C.get("GP")

        @block.vector
        def _(v):
            v.wait_ge(sLD, n_init_ld)
            v.memset(quarter_col, 0.25)
            v.tensor_scalar(out=b1c, in0=b1c, scalar1=256.0, scalar2=None, op0=ALU.mult)
            v.tensor_scalar(out=b2c, in0=b2c, scalar1=256.0, scalar2=None, op0=ALU.mult)
            v.tensor_scalar(out=V(A6, 4096, F32, D)[0:1, :], in0=V(A6, 4096, F32, D)[0:1, :],
                            scalar1=0.5, scalar2=None, op0=ALU.mult).then_inc(sDV, 1)
        C.inc("DV", 1)
        p0_dv = C.get("DV")

        @block.tensor
        def _(t):
            t.wait_ge(sDV, p0_dv)
            for half in range(2):
                mm = t.matmul(PSA[:, half * 512:(half + 1) * 512], ones128_t[0:1, :],
                              V(A6, 0, F32, D)[0:1, half * 512:(half + 1) * 512],
                              start=True, stop=True)
                mm = t.matmul(PSA[:, 1024 + half * 512:1024 + (half + 1) * 512], ones128_t[0:1, :],
                              V(A6, 4096, F32, D)[0:1, half * 512:(half + 1) * 512],
                              start=True, stop=True)
            mm.then_inc(sPE, 1)
        C.inc("PE", 1)
        p0_pe = C.get("PE")

        @block.vector
        def _(v):
            v.wait_ge(sPE, p0_pe)
            v.tensor_copy(WS_bc, PSA[:, 0:1024])
            v.tensor_copy(B0H_bc, PSA[:, 1024:2048]).then_inc(sDV, 1)
        C.inc("DV", 1)

        w8_dv0 = C.get("DV")
        for grp in range(4):
            @block.tensor
            def _(t, grp=grp):
                if grp == 0:
                    t.wait_ge(sGP, w8_gp)
                    t.wait_ge(sDV, C.get("DV"))
                else:
                    t.wait_ge(sDV, C.get("DV"))
                for k in range(16):
                    idx = grp * 16 + k
                    c, j = idx // DC, idx % DC
                    mm = t.transpose(PSB_bf[:, k * 128:(k + 1) * 128],
                                     w8stage[:, c * D + j * 128: c * D + (j + 1) * 128],
                                     identb_t[:, :])
                    if k == 15:
                        mm.then_inc(sPE, 1)
            C.inc("PE", 1)

            @block.vector
            def _(v, grp=grp):
                v.wait_ge(sPE, C.get("PE"))
                last = None
                for k in range(16):
                    idx = grp * 16 + k
                    c, j = idx // DC, idx % DC
                    last = v.tensor_copy(W8T[:, j * D + c * 128: j * D + (c + 1) * 128],
                                         PSB_bf[:, k * 128:(k + 1) * 128])
                last.then_inc(sDV, 1)
            C.inc("DV", 1)
        w8t_dv = C.get("DV")

        xres = V(A1, 0, F32, NT * D)

        u_f32 = V(A6, 0, F32, D)
        sm_i32 = V(A6, 6144, I32, D)
        ub_bf = V(A6, 4096, BF16, D)
        uT_stage = V(A6, 6144, BF16, DC * 512)
        ep_f32 = V(A6, 6144, F32, D)

        def emit_ln(v, src, dst):
            v.tensor_reduce(ln_sc[:, 0:1], src, mybir.AxisListType.X, ALU.add)
            v.tensor_tensor(out=dst, in0=src, in1=src, op=ALU.mult)
            v.tensor_reduce(ln_sc[:, 1:2], dst, mybir.AxisListType.X, ALU.add)
            v.drain()
            v.tensor_scalar(out=ln_sc[:, 2:3], in0=ln_sc[:, 0:1], scalar1=1.0 / D,
                            scalar2=None, op0=ALU.mult)
            v.drain()
            v.tensor_tensor(out=ln_sc[:, 9:10], in0=ln_sc[:, 2:3], in1=ln_sc[:, 2:3], op=ALU.mult)
            v.tensor_scalar(out=ln_sc[:, 3:4], in0=ln_sc[:, 1:2], scalar1=1.0 / D,
                            scalar2=None, op0=ALU.mult)
            v.drain()
            v.tensor_tensor(out=ln_sc[:, 3:4], in0=ln_sc[:, 3:4], in1=ln_sc[:, 9:10], op=ALU.subtract)
            v.drain()
            v.tensor_scalar(out=ln_sc[:, 3:4], in0=ln_sc[:, 3:4], scalar1=LN_EPS,
                            scalar2=None, op0=ALU.add)
            v.drain()
            return v.reciprocal(ln_sc[:, 9:10], ln_sc[:, 3:4])

        def emit_rsqrt_refine(v, rcol, acol, t1, t2):
            v.tensor_tensor(out=t1, in0=rcol, in1=rcol, op=ALU.mult)
            v.drain()
            v.tensor_tensor(out=t1, in0=t1, in1=acol, op=ALU.mult)
            v.drain()
            v.tensor_scalar(out=t1, in0=t1, scalar1=-0.5, scalar2=1.5,
                            op0=ALU.mult, op1=ALU.add)
            v.drain()
            v.tensor_tensor(out=rcol, in0=rcol, in1=t1, op=ALU.mult)
            v.drain()

        def l0ffn_phase(tag):
            import sys
            dbg = globals().get("DBG_STEP", "full")
            steps = ["ln", "quant", "transpose", "mm", "ep1", "ep2", "ep3", "full"].index(dbg)
            for i in range(NT):
                xt = xres[:, i * D:(i + 1) * D]
                @block.vector
                def _(v, i=i, xt=xt):
                    if i > 0:
                        v.wait_ge(sPE, C.get("PE"))
                    emit_ln(v, xt, u_f32).then_inc(sDV, 1)
                C.inc("DV", 1)

                if steps < 1:
                    continue
                @block.scalar
                def _(sc):
                    sc.wait_ge(sDV, C.get("DV"))
                    sc.activation(ln_sc[:, 4:5], ln_sc[:, 9:10], AF.Sqrt).then_inc(sAC, 1)
                C.inc("AC", 1)

                @block.vector
                def _(v, xt=xt):
                    v.wait_ge(sAC, C.get("AC"))
                    emit_rsqrt_refine(v, ln_sc[:, 4:5], ln_sc[:, 3:4],
                                      ln_sc[:, 9:10], ln_sc[:, 10:11])
                    v.tensor_scalar(out=u_f32, in0=xt, scalar1=ln_sc[:, 2:3],
                                    scalar2=ln_sc[:, 4:5], op0=ALU.subtract, op1=ALU.mult)
                    v.reduce_max(ln_sc[:, 5:6], u_f32, axis=mybir.AxisListType.X,
                                 apply_absolute_value=True)
                    v.drain()
                    v.tensor_scalar(out=ln_sc[:, 6:7], in0=ln_sc[:, 5:6],
                                    scalar1=L0FFN_EPS, scalar2=1.0 / 127.0,
                                    op0=ALU.max, op1=ALU.mult)
                    v.drain()
                    v.reciprocal(ln_sc[:, 7:8], ln_sc[:, 6:7])
                    v.tensor_scalar(out=ln_sc[:, 8:9], in0=ln_sc[:, 6:7],
                                    scalar1=0.5, scalar2=None, op0=ALU.mult)
                    v.drain()
                    v.tensor_scalar(out=sm_i32, in0=u_f32, scalar1=ln_sc[:, 7:8],
                                    scalar2=None, op0=ALU.mult)
                    v.drain()
                    v.tensor_copy(ub_bf, sm_i32).then_inc(sDV, 1)
                C.inc("DV", 1)
                q_dv = C.get("DV")

                if steps < 2:
                    continue
                @block.tensor
                def _(t, i=i, q_dv=q_dv):
                    t.wait_ge(sDV, q_dv)
                    for j in range(DC):
                        mm = t.transpose(PSB_bf[:, j * 128:(j + 1) * 128],
                                         ub_bf[:, j * 128:(j + 1) * 128],
                                         identb_t[:, :])
                    mm.then_inc(sPE, 1)
                C.inc("PE", 1)
                tp_pe = C.get("PE")

                @block.vector
                def _(v, tp_pe=tp_pe):
                    v.wait_ge(sPE, tp_pe)
                    v.tensor_copy(uT_stage[:, 0:1024], PSB_bf[:, 0:1024]).then_inc(sDV, 1)
                C.inc("DV", 1)
                ev_dv = C.get("DV")

                if steps < 3:
                    continue
                @block.tensor
                def _(t, ev_dv=ev_dv):
                    t.wait_ge(sDV, ev_dv)
                    for j in range(DC):
                        lhsT = uT_stage[:, j * 128:(j + 1) * 128]
                        last = j == DC - 1
                        mm0 = t.matmul(PSA[:, 0:512], lhsT, W8T[:, j * D:j * D + 512],
                                       start=(j == 0), stop=last)
                        mm1 = t.matmul(PSA[:, 512:1024], lhsT, W8T[:, j * D + 512:(j + 1) * D],
                                       start=(j == 0), stop=last)
                        if last:
                            mm1.then_inc(sPE, 1)
                C.inc("PE", 1)
                mm_pe = C.get("PE")

                if steps < 4:
                    continue
                @block.vector
                def _(v, xt=xt, mm_pe=mm_pe, steps=steps):
                    v.wait_ge(sPE, mm_pe)
                    last = v.tensor_scalar(out=ep_f32, in0=PSA[:, 0:1024],
                                           scalar1=ln_sc[:, 8:9], scalar2=None, op0=ALU.mult)
                    if steps >= 5:
                        last = v.tensor_tensor(out=ep_f32, in0=ep_f32, in1=WS_bc, op=ALU.mult)
                    if steps >= 6:
                        last = v.tensor_tensor(out=ep_f32, in0=ep_f32, in1=B0H_bc, op=ALU.add)
                    if steps >= 7:
                        last = v.tensor_tensor(out=xt, in0=xt, in1=ep_f32, op=ALU.add)
                    last.then_inc(sDV, 1)
                C.inc("DV", 1)


        def tern_prep():
            tstage = V(A5, 24576, F32, 1024)
            tstage_i = V(A5, 28672, I32, 1024)
            tbf = V(A6, 4096, BF16, D)
            for mi, nm in enumerate(("q", "k", "v", "o")):
                @block.vector
                def _(v, mi=mi):
                    v.wait_ge(sGP, C.get("GP"))
                    v.memset(ln_sc[:, 12:13], 0.0)
                    v.drain()
                C.inc("DV", 0)
                for st in range(8):
                    @block.sync
                    def _(s, nm=nm, st=st):
                        s.wait_ge(sDV, C.get("DV"))
                        s.dma_start(out=tstage,
                                    in_=bw[nm].ap()[st * 128:(st + 1) * 128, :]).then_inc(sLD, 16)
                    C.inc("LD", 16)
                    @block.vector
                    def _(v):
                        v.wait_ge(sLD, C.get("LD"))
                        v.tensor_reduce(ln_sc[:, 11:12], tstage, mybir.AxisListType.X,
                                        ALU.add, apply_absolute_value=True)
                        v.drain()
                        v.tensor_tensor(out=ln_sc[:, 12:13], in0=ln_sc[:, 12:13],
                                        in1=ln_sc[:, 11:12], op=ALU.add)
                        v.drain()
                        v.engine_nop().then_inc(sDV, 1)
                    C.inc("DV", 1)
                @block.tensor
                def _(t):
                    t.wait_ge(sDV, C.get("DV"))
                    t.matmul(PSB[:, 0:1], ones128_t[:, :], ln_sc[:, 12:13],
                             start=True, stop=True).then_inc(sPE, 1)
                C.inc("PE", 1)
                @block.vector
                def _(v, mi=mi):
                    v.wait_ge(sPE, C.get("PE"))
                    v.tensor_scalar(out=ln_sc[:, 13:14], in0=PSB[:, 0:1],
                                    scalar1=1.0 / (D * D), scalar2=WQ_EPS,
                                    op0=ALU.mult, op1=ALU.max)
                    v.drain()
                    v.tensor_copy(swinv[:, mi:mi + 1], ln_sc[:, 13:14])
                    v.reciprocal(swrec[:, mi:mi + 1], ln_sc[:, 13:14])
                    v.drain()
                    v.engine_nop().then_inc(sDV, 1)
                C.inc("DV", 1)
                wqT_dst = V(A2, mi * 16384, BF16, DC * D)
                for er in range(8):
                    @block.sync
                    def _(s, nm=nm, er=er):
                        s.wait_ge(sDV, C.get("DV"))
                        s.dma_start(out=tstage,
                                    in_=bw[nm].ap()[er * 128:(er + 1) * 128, 0:1024]).then_inc(sLD, 16)
                    C.inc("LD", 16)
                    @block.vector
                    def _(v, mi=mi):
                        v.wait_ge(sLD, C.get("LD"))
                        v.wait_ge(sPE, C.get("PE"))
                        v.tensor_scalar(out=tstage_i, in0=tstage, scalar1=swrec[:, mi:mi + 1],
                                        scalar2=None, op0=ALU.mult)
                        v.drain()
                        v.tensor_scalar(out=tbf, in0=tstage_i, scalar1=-1.0, scalar2=1.0,
                                        op0=ALU.max, op1=ALU.min)
                        v.engine_nop().then_inc(sDV, 1)
                    C.inc("DV", 1)
                    @block.tensor
                    def _(t):
                        t.wait_ge(sDV, C.get("DV"))
                        for j in range(DC):
                            mm = t.transpose(PSB_bf[:, j * 128:(j + 1) * 128],
                                             tbf[:, j * 128:(j + 1) * 128], identb_t[:, :])
                        mm.then_inc(sPE, 1)
                    C.inc("PE", 1)
                    @block.vector
                    def _(v, er=er, wqT_dst=wqT_dst):
                        v.wait_ge(sPE, C.get("PE"))
                        dst = wqT_dst.rearrange("p (dj e) -> p dj e", dj=DC)[:, :, er * 128:(er + 1) * 128]
                        v.tensor_copy(dst, PSB_bf[:, 0:1024].rearrange("p (dj e) -> p dj e", dj=DC)).then_inc(sDV, 1)
                    C.inc("DV", 1)
                @block.sync
                def _(s, nm=nm, wqT_dst=wqT_dst):
                    s.wait_ge(sDV, C.get("DV"))
                    s.dma_start(out=tern_sp[nm].ap(), in_=wqT_dst).then_inc(sST, 16)
                C.inc("ST", 16)

        def l0attn_wprep():
            tbf = V(A6, 4096, BF16, D)
            for mi, nm in enumerate(("q", "k", "v", "o")):
                wT_dst = V(A2, mi * 16384, BF16, DC * D)
                for er in range(8):
                    @block.gpsimd
                    def _(g, nm=nm, er=er):
                        g.wait_ge(sPE, C.get("PE"))
                        g.dma_start(out=tbf,
                                    in_=aw[nm].ap()[er * 128:(er + 1) * 128, :]).then_inc(sGP, 16)
                    C.inc("GP", 16)
                    @block.tensor
                    def _(t):
                        t.wait_ge(sGP, C.get("GP"))
                        t.wait_ge(sDV, C.get("DV"))
                        for j in range(DC):
                            mm = t.transpose(PSB_bf[:, j * 128:(j + 1) * 128],
                                             tbf[:, j * 128:(j + 1) * 128], identb_t[:, :])
                        mm.then_inc(sPE, 1)
                    C.inc("PE", 1)
                    @block.vector
                    def _(v, er=er, wT_dst=wT_dst):
                        v.wait_ge(sPE, C.get("PE"))
                        v.wait_ge(sST, C.get("ST"))
                        dst = wT_dst.rearrange("p (dj e) -> p dj e", dj=DC)[:, :, er * 128:(er + 1) * 128]
                        v.tensor_copy(dst, PSB_bf[:, 0:1024].rearrange("p (dj e) -> p dj e", dj=DC)).then_inc(sDV, 1)
                    C.inc("DV", 1)

        def emit_tile_ln(xt):
            @block.vector
            def _(v, xt=xt):
                v.wait_ge(sPE, C.get("PE"))
                emit_ln(v, xt, u_f32).then_inc(sDV, 1)
            C.inc("DV", 1)
            @block.scalar
            def _(sc):
                sc.wait_ge(sDV, C.get("DV"))
                sc.activation(ln_sc[:, 4:5], ln_sc[:, 9:10], AF.Sqrt).then_inc(sAC, 1)
            C.inc("AC", 1)
            @block.vector
            def _(v, xt=xt):
                v.wait_ge(sAC, C.get("AC"))
                emit_rsqrt_refine(v, ln_sc[:, 4:5], ln_sc[:, 3:4],
                                  ln_sc[:, 9:10], ln_sc[:, 10:11])
                v.tensor_scalar(out=u_f32, in0=xt, scalar1=ln_sc[:, 2:3],
                                scalar2=ln_sc[:, 4:5], op0=ALU.subtract, op1=ALU.mult).then_inc(sDV, 1)
            C.inc("DV", 1)

        def emit_transpose_to_stage(src_bf, tg):
            @block.tensor
            def _(t, src_bf=src_bf):
                t.wait_ge(sDV, C.get("DV"))
                for j in range(DC):
                    mm = t.transpose(PSB_bf[:, 2048 + j * 128:2048 + (j + 1) * 128],
                                     src_bf[:, j * 128:(j + 1) * 128], identb_t[:, :])
                mm.then_inc(sPE, 1)
            C.inc("PE", 1)
            @block.vector
            def _(v, tg=tg):
                v.wait_ge(sPE, C.get("PE"))
                dst = uT_stage.rearrange("p (j t) -> p j t", j=DC)[:, :, tg * 128:(tg + 1) * 128]
                v.tensor_copy(dst, PSB_bf[:, 2048:3072].rearrange("p (j t) -> p j t", j=DC)).then_inc(sDV, 1)
            C.inc("DV", 1)

        def l0attn_phase():
            WT = {nm: V(A2, mi * 16384, BF16, DC * D)
                  for mi, nm in enumerate(("q", "k", "v", "o"))}
            qTsb = V(A4, 0, BF16, DC * TL)
            K_loc = V(A6, 0, BF16, D)
            V_loc = V(A6, 2048, BF16, D)
            fmA = V(A6, 4096, F32, 512)
            BKb = V(A5, 24576, F32, D)
            BVb = V(A5, 28672, F32, D)
            attnN = V(A5, 24576, BF16, DC * 512)
            tmp64 = V(A6, 4096, F32, 512)

            @block.sync
            def _(s):
                s.wait_ge(sDV, C.get("DV"))
                s.dma_start(out=V(A6, 0, F32, D)[0:1, :], in_=ab["k"].ap().unsqueeze(0)).then_inc(sLD, 16)
                s.dma_start(out=V(A6, 4096, F32, D)[0:1, :], in_=ab["v"].ap().unsqueeze(0)).then_inc(sLD, 16)
            C.inc("LD", 32)
            @block.tensor
            def _(t):
                t.wait_ge(sLD, C.get("LD"))
                for half in range(2):
                    t.matmul(PSB[:, half * 512:(half + 1) * 512], ones128_t[0:1, :],
                             V(A6, 0, F32, D)[0:1, half * 512:(half + 1) * 512],
                             start=True, stop=True)
                    mm = t.matmul(PSB[:, 1024 + half * 512:1024 + (half + 1) * 512], ones128_t[0:1, :],
                                  V(A6, 4096, F32, D)[0:1, half * 512:(half + 1) * 512],
                                  start=True, stop=True)
                mm.then_inc(sPE, 1)
            C.inc("PE", 1)
            @block.vector
            def _(v):
                v.wait_ge(sPE, C.get("PE"))
                v.tensor_copy(BKb, PSB[:, 0:1024])
                v.tensor_copy(BVb, PSB[:, 1024:2048]).then_inc(sDV, 1)
            C.inc("DV", 1)

            for g in range(NG):
                for tg in range(4):
                    i = g * 4 + tg
                    xt = xres[:, i * D:(i + 1) * D]
                    emit_tile_ln(xt)
                    @block.vector
                    def _(v):
                        v.tensor_copy(ub_bf, u_f32).then_inc(sDV, 1)
                    C.inc("DV", 1)
                    emit_transpose_to_stage(ub_bf, tg)
                for ec in range(DC):
                    @block.tensor
                    def _(t, ec=ec):
                        t.wait_ge(sDV, C.get("DV"))
                        for j in range(DC):
                            mm = t.matmul(PSA[:, 0:512],
                                          WT["q"][:, j * D + ec * 128:j * D + (ec + 1) * 128],
                                          uT_stage[:, j * 512:(j + 1) * 512],
                                          start=(j == 0), stop=(j == DC - 1))
                        mm.then_inc(sPE, 1)
                    C.inc("PE", 1)
                    @block.vector
                    def _(v, ec=ec):
                        v.wait_ge(sPE, C.get("PE"))
                        v.tensor_scalar(out=fmA, in0=PSA[:, 0:512],
                                        scalar1=bq_col[:, ec:ec + 1], scalar2=None, op0=ALU.add)
                        v.tensor_scalar(out=PSB[:, 1536:2048], in0=fmA, scalar1=0.0,
                                        scalar2=None, op0=ALU.max)
                        v.tensor_scalar(out=PSB[:, 1024:1536], in0=fmA, scalar1=0.0,
                                        scalar2=None, op0=ALU.min).then_inc(sDV, 1)
                    C.inc("DV", 1)
                    @block.scalar
                    def _(sc):
                        sc.wait_ge(sDV, C.get("DV"))
                        sc.activation(fmA, PSB[:, 1024:1536], AF.Exp).then_inc(sAC, 1)
                    C.inc("AC", 1)
                    @block.vector
                    def _(v, ec=ec, g=g):
                        v.wait_ge(sAC, C.get("AC"))
                        v.tensor_tensor(out=qTsb[:, ec * TL + g * 512: ec * TL + g * 512 + 512],
                                        in0=fmA, in1=PSB[:, 1536:2048], op=ALU.add).then_inc(sDV, 1)
                    C.inc("DV", 1)
                for tg in range(4):
                    i = g * 4 + tg
                    for which in ("k", "v"):
                        @block.tensor
                        def _(t, tg=tg, which=which):
                            t.wait_ge(sDV, C.get("DV"))
                            for j in range(DC):
                                lhsT = uT_stage[:, j * 512 + tg * 128: j * 512 + (tg + 1) * 128]
                                mm0 = t.matmul(PSB[:, 0:512], lhsT,
                                               WT[which][:, j * D:j * D + 512],
                                               start=(j == 0), stop=(j == DC - 1))
                                mm1 = t.matmul(PSB[:, 512:1024], lhsT,
                                               WT[which][:, j * D + 512:(j + 1) * D],
                                               start=(j == 0), stop=(j == DC - 1))
                            mm1.then_inc(sPE, 1)
                        C.inc("PE", 1)
                        if which == "k":
                            for half in range(2):
                                @block.vector
                                def _(v, half=half):
                                    v.wait_ge(sPE, C.get("PE"))
                                    v.tensor_tensor(out=fmA, in0=PSB[:, half * 512:(half + 1) * 512],
                                                    in1=BKb[:, half * 512:(half + 1) * 512], op=ALU.add)
                                    v.tensor_scalar(out=PSB[:, 1536:2048], in0=fmA, scalar1=0.0,
                                                    scalar2=None, op0=ALU.max)
                                    v.tensor_scalar(out=PSB[:, 1024:1536], in0=fmA, scalar1=0.0,
                                                    scalar2=None, op0=ALU.min).then_inc(sDV, 1)
                                C.inc("DV", 1)
                                @block.scalar
                                def _(sc):
                                    sc.wait_ge(sDV, C.get("DV"))
                                    sc.activation(fmA, PSB[:, 1024:1536], AF.Exp).then_inc(sAC, 1)
                                C.inc("AC", 1)
                                @block.vector
                                def _(v, half=half):
                                    v.wait_ge(sAC, C.get("AC"))
                                    v.tensor_tensor(out=K_loc[:, half * 512:(half + 1) * 512],
                                                    in0=fmA, in1=PSB[:, 1536:2048], op=ALU.add).then_inc(sDV, 1)
                                C.inc("DV", 1)
                        else:
                            @block.vector
                            def _(v):
                                v.wait_ge(sPE, C.get("PE"))
                                v.tensor_tensor(out=V_loc, in0=PSB[:, 0:1024], in1=BVb,
                                                op=ALU.add).then_inc(sDV, 1)
                            C.inc("DV", 1)
                    @block.tensor
                    def _(t, i=i):
                        t.wait_ge(sDV, C.get("DV"))
                        for h in range(H):
                            hp, rs = h // 2, (h % 2) * 64
                            t.matmul(PSA[rs:rs + 64, 1024 + hp * 64:1024 + (hp + 1) * 64],
                                     K_loc[:, h * 64:(h + 1) * 64], V_loc[:, h * 64:(h + 1) * 64],
                                     start=(i == 0 and h == 0), stop=(i == NT - 1 and h == H - 1),
                                     skip_group_check=True)
                            mm = t.matmul(PSA[rs:rs + 64, 1536 + hp:1536 + hp + 1],
                                          K_loc[:, h * 64:(h + 1) * 64], onescol_t[:, :],
                                          start=(i == 0 and h == 0), stop=(i == NT - 1 and h == H - 1),
                                          skip_group_check=True)
                        mm.then_inc(sPE, 1)
                    C.inc("PE", 1)
            kv_sb = V(A6, 6144, F32, 520)
            @block.vector
            def _(v):
                v.wait_ge(sPE, C.get("PE"))
                v.tensor_copy(kv_sb[:, 0:512], PSA[:, 1024:1536])
                v.tensor_copy(kv_sb[:, 512:520], PSA[:, 1536:1544])
                v.drain()
                v.engine_nop().then_inc(sDV, 1)
            C.inc("DV", 1)
            @block.gpsimd
            def _(g_):
                g_.wait_ge(sDV, C.get("DV"))
                g_.dma_start(out=kv_in.ap()[:, 0:520], in_=kv_sb).then_inc(sGP, 16)
                g_.wait_ge(sGP, C.inc("GP", 16))
                g_.collective_compute(
                    "AllReduce", ALU.add, replica_groups=[list(range(NCORE))],
                    ins=[kv_in.ap().opt()], outs=[kv_out.ap().opt()]).then_inc(sCC, 1)
            C.inc("CC", 1)
            @block.sync
            def _(s):
                s.wait_ge(sCC, C.get("CC"))
                s.dma_start(out=kv_sb, in_=kv_out.ap()[:, 0:520]).then_inc(sLD, 16)
            C.inc("LD", 16)
            kvx = V(A6, 8224, BF16, 520)
            @block.vector
            def _(v):
                v.wait_ge(sLD, C.get("LD"))
                kvx3 = kvx.rearrange("p (hp c) -> p hp c", hp=8)
                v.tensor_copy(kvx3[:, :, 0:64], kv_sb[:, 0:512].rearrange("p (hp c) -> p hp c", hp=8))
                v.tensor_copy(kvx3[:, :, 64:65], kv_sb[:, 512:520].rearrange("p (hp c) -> p hp c", hp=8)).then_inc(sDV, 1)
            C.inc("DV", 1)
            for qb in range(4):
                for h in range(H):
                    hp, rs = h // 2, (h % 2) * 64
                    @block.tensor
                    def _(t, hp=hp, rs=rs, h=h, qb=qb):
                        t.wait_ge(sDV, C.get("DV"))
                        t.matmul(PSB[0:65, 0:512], kvx[rs:rs + 64, hp * 65:(hp + 1) * 65],
                                 qTsb[rs:rs + 64, (h // 2) * TL + qb * 512:(h // 2) * TL + qb * 512 + 512],
                                 start=True, stop=True).then_inc(sPE, 1)
                    C.inc("PE", 1)
                    @block.vector
                    def _(v):
                        v.wait_ge(sPE, C.get("PE"))
                        v.tensor_scalar(out=V(A6, 9472, F32, 512)[0:1, :], in0=PSB[64:65, 0:512],
                                        scalar1=L0ATTN_EPS, scalar2=None, op0=ALU.add)
                        v.drain()
                        v.reciprocal(V(A6, 11520, F32, 512)[0:1, :], V(A6, 9472, F32, 512)[0:1, :])
                        v.drain()
                        v.engine_nop().then_inc(sDV, 1)
                    C.inc("DV", 1)
                    @block.tensor
                    def _(t):
                        t.wait_ge(sDV, C.get("DV"))
                        t.matmul(PSB[0:64, 512:1024], ones128_t[0:1, 0:64],
                                 V(A6, 11520, F32, 512)[0:1, :], start=True, stop=True).then_inc(sPE, 1)
                    C.inc("PE", 1)
                    @block.vector
                    def _(v, h=h, rs=rs):
                        v.wait_ge(sPE, C.get("PE"))
                        v.tensor_copy(tmp64[0:64, :], PSB[0:64, 0:512])
                        v.tensor_tensor(out=attnN[rs:rs + 64, (h // 2) * 512:(h // 2) * 512 + 512],
                                        in0=tmp64[0:64, :], in1=PSB[0:64, 512:1024],
                                        op=ALU.mult).then_inc(sDV, 1)
                    C.inc("DV", 1)
                for oc in range(DC):
                    @block.tensor
                    def _(t, oc=oc):
                        t.wait_ge(sDV, C.get("DV"))
                        for j in range(DC):
                            mm = t.matmul(PSA[:, 0:512],
                                          WT["o"][:, j * D + oc * 128:j * D + (oc + 1) * 128],
                                          attnN[:, j * 512:(j + 1) * 512],
                                          start=(j == 0), stop=(j == DC - 1))
                        mm.then_inc(sPE, 1)
                    C.inc("PE", 1)
                    @block.vector
                    def _(v, oc=oc):
                        v.wait_ge(sPE, C.get("PE"))
                        v.tensor_scalar(out=V(A6, 9472, F32, 512), in0=PSA[:, 0:512],
                                        scalar1=bo_col[:, oc:oc + 1], scalar2=None,
                                        op0=ALU.add).then_inc(sDV, 1)
                    C.inc("DV", 1)
                    @block.tensor
                    def _(t):
                        t.wait_ge(sDV, C.get("DV"))
                        for sub in range(4):
                            mm = t.transpose(PSB[:, 512 + sub * 128:512 + (sub + 1) * 128],
                                             V(A6, 9472, F32, 512)[:, sub * 128:(sub + 1) * 128],
                                             ident32_t[:, :])
                        mm.then_inc(sPE, 1)
                    C.inc("PE", 1)
                    @block.vector
                    def _(v, qb=qb, oc=oc):
                        v.wait_ge(sPE, C.get("PE"))
                        xr3 = xres.rearrange("p (t d) -> p t d", t=NT)
                        dst = xr3[:, qb * 4:(qb + 1) * 4, oc * 128:(oc + 1) * 128]
                        v.tensor_tensor(out=dst, in0=dst,
                                        in1=PSB[:, 512:1024].rearrange("p (s d) -> p s d", s=4),
                                        op=ALU.add).then_inc(sDV, 1)
                    C.inc("DV", 1)

        def l1ffn_phase():
            w2sb = V(A2, 0, BF16, FC * D)
            w1buf = [V(A5, 0, BF16, DC * 128), V(A5, 2048, BF16, DC * 128)]
            hqT = V(A4, 0, BF16, FC * 512)
            P1 = V(A6, 0, F32, 512)
            P2 = V(A6, 2048, F32, 512)
            P2i = V(A6, 2048, I32, 512)
            P3f = V(A6, 4096, F32, 512)
            P3i = V(A6, 4096, I32, 512)
            @block.gpsimd
            def _(g_):
                g_.wait_ge(sDV, C.get("DV"))
                g_.wait_ge(sPE, C.get("PE"))
                g_.dma_start(out=w2sb.rearrange("p (j o) -> p j o", j=FC),
                             in_=w2.ap().rearrange("(j p) o -> p j o", p=128)).then_inc(sGP, 16)
            C.inc("GP", 16)
            for g in range(NG):
                for tg in range(4):
                    i = g * 4 + tg
                    xt = xres[:, i * D:(i + 1) * D]
                    emit_tile_ln(xt)
                    @block.vector
                    def _(v):
                        v.tensor_copy(ub_bf, u_f32).then_inc(sDV, 1)
                    C.inc("DV", 1)
                    emit_transpose_to_stage(ub_bf, tg)
                if g == 0:
                    w1last = [None, None]
                for fc in range(FC):
                    par = fc % 2
                    wv = w1last[par] if w1last[par] is not None else C.get("PE")
                    @block.gpsimd
                    def _(g_, fc=fc, par=par, wv=wv):
                        g_.wait_ge(sPE, wv)
                        g_.dma_start(out=w1buf[par].rearrange("p (j f) -> p j f", j=DC),
                                     in_=w1.ap()[:, fc * 128:(fc + 1) * 128]
                                     .rearrange("(j p) f -> p j f", p=128)).then_inc(sGP, 16)
                    C.inc("GP", 16)
                    gp_after_load = C.get("GP")
                    bank = fc % 2
                    @block.tensor
                    def _(t, fc=fc, par=par, bank=bank, gp_after_load=gp_after_load):
                        t.wait_ge(sGP, gp_after_load)
                        t.wait_ge(sDV, C.get("DV"))
                        for j in range(DC):
                            mm = t.matmul(PSA[:, bank * 512:(bank + 1) * 512],
                                          w1buf[par][:, j * 128:(j + 1) * 128],
                                          uT_stage[:, j * 512:(j + 1) * 512],
                                          start=(j == 0), stop=(j == DC - 1))
                        mm.then_inc(sPE, 1)
                    C.inc("PE", 1)
                    w1last[par] = C.get("PE")
                    pe_h = C.get("PE")
                    @block.vector
                    def _(v, fc=fc, bank=bank, pe_h=pe_h):
                        v.wait_ge(sPE, pe_h)
                        v.tensor_scalar(out=P3i, in0=PSA[:, bank * 512:(bank + 1) * 512],
                                        scalar1=256.0, scalar2=b1c[:, fc:fc + 1],
                                        op0=ALU.mult, op1=ALU.add)
                        v.drain()
                        v.tensor_scalar(out=P1, in0=P3i, scalar1=1.0 / 64, scalar2=None,
                                        op0=ALU.mult)
                        v.drain()
                        v.tensor_scalar(out=P2i, in0=P1, scalar1=-0.5, scalar2=None,
                                        op0=ALU.add)
                        v.drain()
                        v.engine_nop().then_inc(sDV, 1)
                    C.inc("DV", 1)
                    @block.scalar
                    def _(sc):
                        sc.wait_ge(sDV, C.get("DV"))
                        sc.activation(PSB[:, 0:512], P2i, AF.Gelu, scale=0.25)
                        sc.activation(PSB[:, 512:1024], P2i, AF.Gelu, scale=0.25,
                                      bias=quarter_col).then_inc(sAC, 1)
                    C.inc("AC", 1)
                    @block.vector
                    def _(v, fc=fc):
                        v.wait_ge(sAC, C.get("AC"))
                        v.tensor_tensor(out=P3f, in0=P1, in1=P2i, op=ALU.subtract)
                        v.tensor_copy(P2, PSB[:, 0:512])
                        v.drain()
                        v.tensor_tensor(out=P1, in0=PSB[:, 512:1024], in1=P2, op=ALU.subtract)
                        v.drain()
                        v.tensor_tensor(out=PSB[:, 0:512], in0=P3f, in1=P1, op=ALU.mult)
                        v.drain()
                        v.tensor_tensor(out=P1, in0=PSB[:, 0:512], in1=P2, op=ALU.add)
                        v.drain()
                        v.tensor_scalar(out=P3i, in0=P1, scalar1=256.0, scalar2=None,
                                        op0=ALU.mult)
                        v.drain()
                        v.tensor_scalar(out=hqT[:, fc * 512:(fc + 1) * 512], in0=P3i,
                                        scalar1=1.0 / 256, scalar2=None,
                                        op0=ALU.mult).then_inc(sDV, 1)
                    C.inc("DV", 1)
                for oc in range(DC):
                    @block.tensor
                    def _(t, oc=oc):
                        t.wait_ge(sDV, C.get("DV"))
                        t.wait_ge(sGP, C.get("GP"))
                        for j2 in range(FC):
                            mm = t.matmul(PSA[:, 1024:1536],
                                          w2sb[:, j2 * D + oc * 128:j2 * D + (oc + 1) * 128],
                                          hqT[:, j2 * 512:(j2 + 1) * 512],
                                          start=(j2 == 0), stop=(j2 == FC - 1))
                        mm.then_inc(sPE, 1)
                    C.inc("PE", 1)
                    @block.vector
                    def _(v, oc=oc):
                        v.wait_ge(sPE, C.get("PE"))
                        v.tensor_scalar(out=P3i, in0=PSA[:, 1024:1536], scalar1=256.0,
                                        scalar2=b2c[:, oc:oc + 1], op0=ALU.mult, op1=ALU.add)
                        v.drain()
                        v.tensor_scalar(out=P1, in0=P3i, scalar1=1.0 / 512, scalar2=None,
                                        op0=ALU.mult).then_inc(sDV, 1)
                    C.inc("DV", 1)
                    @block.tensor
                    def _(t):
                        t.wait_ge(sDV, C.get("DV"))
                        for sub in range(4):
                            mm = t.transpose(PSB[:, 1024 + sub * 128:1024 + (sub + 1) * 128],
                                             P1[:, sub * 128:(sub + 1) * 128], ident32_t[:, :])
                        mm.then_inc(sPE, 1)
                    C.inc("PE", 1)
                    @block.vector
                    def _(v, g=g, oc=oc):
                        v.wait_ge(sPE, C.get("PE"))
                        xr3 = xres.rearrange("p (t d) -> p t d", t=NT)
                        dst = xr3[:, g * 4:(g + 1) * 4, oc * 128:(oc + 1) * 128]
                        v.tensor_tensor(out=dst, in0=dst,
                                        in1=PSB[:, 1024:1536].rearrange("p (s d) -> p s d", s=4),
                                        op=ALU.add).then_inc(sDV, 1)
                    C.inc("DV", 1)

        def l1attn_phase():
            WT = {nm: V(A2, mi * 16384, BF16, DC * D)
                  for mi, nm in enumerate(("q", "k", "v", "o"))}
            qTsb = V(A4, 0, BF16, DC * TL)
            kTsb = V(A5, 0, BF16, DC * TL)
            V_tiles = [V(A6, 0, BF16, H * 65), V(A6, 2176, BF16, H * 65)]
            for mi, nm in enumerate(("q", "k", "v", "o")):
                @block.sync
                def _(s, mi=mi, nm=nm):
                    s.wait_ge(sDV, C.get("DV"))
                    s.wait_ge(sPE, C.get("PE"))
                    s.dma_start(out=V(A2, mi * 16384, BF16, DC * D),
                                in_=tern_sp[nm].ap()).then_inc(sLD, 16)
            C.inc("LD", 16 * 4)
            ld_tern = C.get("LD")
            for g in range(NG):
                for tg in range(4):
                    i = g * 4 + tg
                    xt = xres[:, i * D:(i + 1) * D]
                    if g > 0 and tg == 0:
                        @block.vector
                        def _(v):
                            v.wait_ge(sGP, C.get("GP"))
                    emit_tile_ln(xt)
                    @block.vector
                    def _(v):
                        v.wait_ge(sPE, C.get("PE"))
                        PSB_f = PSB.ap()
                        v.tensor_tensor(out=PSB_f[:, 1024:1536], in0=u_f32[:, 0:512],
                                        in1=u_f32[:, 0:512], op=ALU.mult)
                        v.tensor_reduce(ln_sc[:, 11:12], PSB_f[:, 1024:1536],
                                        mybir.AxisListType.X, ALU.add)
                        v.tensor_tensor(out=PSB_f[:, 1024:1536], in0=u_f32[:, 512:1024],
                                        in1=u_f32[:, 512:1024], op=ALU.mult)
                        v.tensor_reduce(ln_sc[:, 14:15], PSB_f[:, 1024:1536],
                                        mybir.AxisListType.X, ALU.add)
                        v.drain()
                        v.tensor_tensor(out=ln_sc[:, 11:12], in0=ln_sc[:, 11:12],
                                        in1=ln_sc[:, 14:15], op=ALU.add)
                        v.drain()
                        v.tensor_scalar(out=ln_sc[:, 11:12], in0=ln_sc[:, 11:12],
                                        scalar1=1.0 / D, scalar2=RMS_EPS,
                                        op0=ALU.mult, op1=ALU.add)
                        v.drain()
                        v.reciprocal(ln_sc[:, 12:13], ln_sc[:, 11:12])
                        v.drain()
                        v.engine_nop().then_inc(sDV, 1)
                    C.inc("DV", 1)
                    @block.scalar
                    def _(sc):
                        sc.wait_ge(sDV, C.get("DV"))
                        sc.activation(ln_sc[:, 12:13], ln_sc[:, 12:13], AF.Sqrt).then_inc(sAC, 1)
                    C.inc("AC", 1)
                    @block.vector
                    def _(v):
                        v.wait_ge(sAC, C.get("AC"))
                        emit_rsqrt_refine(v, ln_sc[:, 12:13], ln_sc[:, 11:12],
                                          ln_sc[:, 9:10], ln_sc[:, 10:11])
                        v.reduce_max(ln_sc[:, 5:6], u_f32, axis=mybir.AxisListType.X,
                                     apply_absolute_value=True)
                        v.drain()
                        v.tensor_tensor(out=ln_sc[:, 5:6], in0=ln_sc[:, 5:6],
                                        in1=ln_sc[:, 12:13], op=ALU.mult)
                        v.drain()
                        v.tensor_scalar(out=ln_sc[:, 6:7], in0=ln_sc[:, 5:6],
                                        scalar1=AQ_EPS, scalar2=1.0 / 127.0,
                                        op0=ALU.max, op1=ALU.mult)
                        v.drain()
                        v.reciprocal(ln_sc[:, 7:8], ln_sc[:, 6:7])
                        v.drain()
                        v.tensor_tensor(out=ln_sc[:, 13:14], in0=ln_sc[:, 12:13],
                                        in1=ln_sc[:, 7:8], op=ALU.mult)
                        v.drain()
                        PSB_i = PSB.ap().bitcast(I32)
                        for hh in range(2):
                            v.tensor_scalar(out=PSB_i[:, 1536:2048], in0=u_f32[:, hh * 512:(hh + 1) * 512],
                                            scalar1=ln_sc[:, 13:14], scalar2=None, op0=ALU.mult)
                            v.drain()
                            v.tensor_scalar(out=ub_bf[:, hh * 512:(hh + 1) * 512], in0=PSB_i[:, 1536:2048],
                                            scalar1=ln_sc[:, 6:7], scalar2=None, op0=ALU.mult)
                            v.drain()
                        v.engine_nop().then_inc(sDV, 1)
                    C.inc("DV", 1)
                    emit_transpose_to_stage(ub_bf, tg)
                @block.tensor
                def _(t):
                    t.wait_ge(sLD, ld_tern)
                for dstsb, wname, sw in ((qTsb, "q", 0), (kTsb, "k", 1)):
                    for ec in range(DC):
                        @block.tensor
                        def _(t, wname=wname, ec=ec):
                            t.wait_ge(sDV, C.get("DV"))
                            for j in range(DC):
                                mm = t.matmul(PSA[:, 0:512],
                                              WT[wname][:, j * D + ec * 128:j * D + (ec + 1) * 128],
                                              uT_stage[:, j * 512:(j + 1) * 512],
                                              start=(j == 0), stop=(j == DC - 1))
                            mm.then_inc(sPE, 1)
                        C.inc("PE", 1)
                        @block.vector
                        def _(v, dstsb=dstsb, ec=ec, g=g, sw=sw):
                            v.wait_ge(sPE, C.get("PE"))
                            v.tensor_scalar(out=dstsb[:, ec * TL + g * 512: ec * TL + g * 512 + 512],
                                            in0=PSA[:, 0:512], scalar1=swinv[:, sw:sw + 1],
                                            scalar2=None, op0=ALU.mult)
                            v.drain()
                            v.engine_nop().then_inc(sDV, 1)
                        C.inc("DV", 1)
                for tg in range(4):
                    i = g * 4 + tg
                    Vt = V_tiles[i % 2]
                    @block.tensor
                    def _(t, tg=tg):
                        t.wait_ge(sDV, C.get("DV"))
                        for j in range(DC):
                            lhsT = uT_stage[:, j * 512 + tg * 128: j * 512 + (tg + 1) * 128]
                            mm0 = t.matmul(PSB[:, 0:512], lhsT,
                                           WT["v"][:, j * D:j * D + 512],
                                           start=(j == 0), stop=(j == DC - 1))
                            mm1 = t.matmul(PSB[:, 512:1024], lhsT,
                                           WT["v"][:, j * D + 512:(j + 1) * D],
                                           start=(j == 0), stop=(j == DC - 1))
                        mm1.then_inc(sPE, 1)
                    C.inc("PE", 1)
                    @block.vector
                    def _(v, Vt=Vt):
                        v.wait_ge(sPE, C.get("PE"))
                        v.wait_ge(sGP, C.get("GP"))
                        v.memset(Vt.rearrange("p (h c) -> p h c", h=H)[:, :, 64:65], 1.0)
                        v.drain()
                        v.tensor_scalar(out=Vt.rearrange("p (h c) -> p h c", h=H)[:, :, 0:64],
                                        in0=PSB[:, 0:1024].rearrange("p (h c) -> p h c", h=H),
                                        scalar1=swinv[:, 2:3], scalar2=None,
                                        op0=ALU.mult)
                        v.drain()
                        v.engine_nop().then_inc(sDV, 1)
                    C.inc("DV", 1)
                    @block.gpsimd
                    def _(g_, i=i, Vt=Vt):
                        g_.wait_ge(sDV, C.get("DV"))
                        g_.dma_start(out=agv_in.ap().rearrange("(t p) c -> p t c", p=128)[:, i, :],
                                     in_=Vt).then_inc(sGP, 16)
                    C.inc("GP", 16)
            if globals().get("DBG_E", 0) == 1:
                dbgE = nc.dram_tensor("dbgE", [128, 6160], F32, kind="ExternalOutput")
                @block.vector
                def _(v):
                    v.wait_ge(sPE, C.get("PE"))
                    v.wait_ge(sDV, C.get("DV"))
                    v.tensor_copy(V(A2, 49152, F32, 2048), qTsb[:, 0:2048])
                    v.tensor_copy(V(A2, 57344, F32, 2048), kTsb[:, 0:2048]).then_inc(sDV, 1)
                C.inc("DV", 1)
                @block.sync
                def _(s):
                    s.wait_ge(sDV, C.get("DV"))
                    s.dma_start(out=dbgE.ap()[:, 0:2048], in_=V(A2, 49152, F32, 2048)).then_inc(sST, 16)
                    s.dma_start(out=dbgE.ap()[:, 2048:4096], in_=V(A2, 57344, F32, 2048)).then_inc(sST, 16)
                    s.dma_start(out=dbgE.ap()[:, 4096:4100], in_=swrec[:, :]).then_inc(sST, 16)
                    s.dma_start(out=dbgE.ap()[:, 4100:4620].bitcast(BF16), in_=V(A6, 0, BF16, H * 65)).then_inc(sST, 16)
                    s.dma_start(out=dbgE.ap()[:, 5000:6024].bitcast(BF16)[:, 0:2048], in_=uT_stage[:, 0:2048]).then_inc(sST, 16)
                C.inc("ST", 16 * 5)

            @block.gpsimd
            def _(g_):
                g_.wait_ge(sDV, C.get("DV"))
                g_.dma_start(out=agk_in.ap().rearrange("(ec p) t -> p ec t", p=128),
                             in_=kTsb.rearrange("p (ec t) -> p ec t", ec=DC)).then_inc(sGP, 16)
                g_.wait_ge(sGP, C.inc("GP", 16))
                g_.collective_compute(
                    "AllGather", ALU.bypass, replica_groups=[list(range(NCORE))],
                    ins=[agk_in.ap().opt()], outs=[agk_out.ap().opt()]).then_inc(sCC, 1)
                g_.collective_compute(
                    "AllGather", ALU.bypass, replica_groups=[list(range(NCORE))],
                    ins=[agv_in.ap().opt()], outs=[agv_out.ap().opt()]).then_inc(sCC, 1)
            C.inc("CC", 2)
            cc_done = C.get("CC")
            @block.sync
            def _(s):
                s.dma_start(out=spill_x.ap(), in_=xres).then_inc(sST, 16)
            C.inc("ST", 16)
            spill_done = C.get("ST")
            KT_win = V(A5, 0, BF16, DC * TL)
            V_win = V(A1, 0, BF16, 16 * H * 65)
            ctx_acc = V(A2, 0, F32, NT * D)
            A_buf = [V(A6, 0, BF16, 2048), V(A6, 4096, BF16, 2048)]
            normbuf = V(A6, 8192, F32, 1024)
            @block.vector
            def _(v):
                v.wait_ge(sST, spill_done)
                v.wait_ge(sGP, C.get("GP"))
                v.memset(ctx_acc, 0.0)
                v.memset(normbuf, 0.0).then_inc(sDV, 1)
            C.inc("DV", 1)
            @block.gpsimd
            def _(g_):
                g_.wait_ge(sDV, C.get("DV"))
                g_.dma_start(out=norm_dram.ap()[:, 0:1024], in_=normbuf[0:16, :]).then_inc(sGP, 16)
                g_.dma_start(out=norm_dram.ap()[:, 1024:2048], in_=normbuf[16:32, :]).then_inc(sGP, 16)
            C.inc("GP", 32)
            for r in range(globals().get("DBG_R", NCORE)):
                @block.sync
                def _(s, r=r):
                    if r == 0:
                        s.wait_ge(sCC, cc_done)
                        s.wait_ge(sDV, C.get("DV"))
                    else:
                        s.wait_ge(sPE, C.get("PE"))
                    s.dma_start(out=KT_win.rearrange("p (ec t) -> p ec t", ec=DC),
                                in_=agk_out.ap()[r * D:(r + 1) * D, :]
                                .rearrange("(ec p) t -> p ec t", p=128)).then_inc(sLD, 16)
                    s.dma_start(out=V_win.rearrange("p (tc c) -> p tc c", tc=16),
                                in_=agv_out.ap()[r * TL:(r + 1) * TL, :]
                                .rearrange("(tc p) c -> p tc c", p=128)).then_inc(sLD, 16)
                C.inc("LD", 32)
                ld_win = C.get("LD")
                for h in range(globals().get("DBG_H", H)):
                    hp, rs = h // 2, (h % 2) * 64
                    ac_base = C.get("AC")
                    pe_base = C.get("PE")
                    NTC = globals().get("DBG_TC", 16)
                    for tc in range(NTC):
                        @block.tensor
                        def _(t, h=h, hp=hp, rs=rs, tc=tc, ld_win=ld_win):
                            if tc == 0:
                                t.wait_ge(sLD, ld_win)
                                t.wait_ge(sDV, C.get("DV"))
                            for qb in range(4):
                                mm = t.matmul(PSA[:, qb * 512:(qb + 1) * 512],
                                              KT_win[rs:rs + 64, hp * TL + tc * 128: hp * TL + (tc + 1) * 128],
                                              qTsb[rs:rs + 64, hp * TL + qb * 512: hp * TL + qb * 512 + 512],
                                              start=True, stop=True)
                            mm.then_inc(sPE, 1)
                        C.inc("PE", 1)
                        @block.scalar
                        def _(sc, tc=tc):
                            sc.wait_ge(sPE, C.get("PE"))
                            sc.activation(A_buf[tc % 2], PSA[:, 0:2048], AF.Exp,
                                          scale=0.125).then_inc(sAC, 1)
                        C.inc("AC", 1)
                        @block.tensor
                        def _(t, h=h, tc=tc):
                            t.wait_ge(sAC, C.get("AC"))
                            if tc > 0:
                                t.wait_ge(sDV, C.get("DV"))
                            for qb in range(4):
                                mm = t.matmul(PSB[0:65, qb * 512:(qb + 1) * 512],
                                              V_win[:, tc * H * 65 + h * 65: tc * H * 65 + (h + 1) * 65],
                                              A_buf[tc % 2][:, qb * 512:(qb + 1) * 512],
                                              start=True, stop=True)
                            mm.then_inc(sPE, 1)
                        C.inc("PE", 1)
                        @block.vector
                        def _(v, h=h, hp=hp, rs=rs, tc=tc):
                            v.wait_ge(sPE, C.get("PE"))
                            if tc == 0:
                                v.wait_ge(sGP, C.get("GP"))
                            for qb in range(4):
                                dst = ctx_acc[rs:rs + 64, hp * TL + qb * 512: hp * TL + qb * 512 + 512]
                                v.tensor_tensor(out=dst, in0=dst, in1=PSB[0:64, qb * 512:(qb + 1) * 512],
                                                op=ALU.add)
                            if tc == 0:
                                v.tensor_copy(normbuf[64:65, :], PSB[64:65, 0:1024])
                                ni = v.tensor_copy(normbuf[96:97, :], PSB[64:65, 1024:2048])
                            else:
                                v.tensor_tensor(out=normbuf[64:65, :], in0=normbuf[64:65, :],
                                                in1=PSB[64:65, 0:1024], op=ALU.add)
                                ni = v.tensor_tensor(out=normbuf[96:97, :], in0=normbuf[96:97, :],
                                                     in1=PSB[64:65, 1024:2048], op=ALU.add)
                            if tc == NTC - 1:
                                v.drain()
                                v.engine_nop().then_inc(sDV, 1)
                            else:
                                ni.then_inc(sDV, 1)
                        C.inc("DV", 1)
                    @block.gpsimd
                    def _(g_, h=h):
                        g_.wait_ge(sDV, C.get("DV"))
                        g_.dma_start(out=norm_dram.ap()[h:h + 1, 0:1024],
                                     in_=normbuf[64:65, :], accum_op=ALU.add).then_inc(sGP, 16)
                        g_.dma_start(out=norm_dram.ap()[h:h + 1, 1024:2048],
                                     in_=normbuf[96:97, :], accum_op=ALU.add).then_inc(sGP, 16)
                    C.inc("GP", 32)
            if globals().get("DBG_E", 0) == 2:
                dbgE2 = nc.dram_tensor("dbgE2", [128, 10240], F32, kind="ExternalOutput")
                @block.vector
                def _(v):
                    v.wait_ge(sPE, C.get("PE"))
                    v.wait_ge(sDV, C.get("DV"))
                    v.tensor_copy(V(A2, 65536 - 16384, F32, 2048), PSA[:, 0:2048])
                    v.tensor_copy(V(A2, 65536 - 8192, F32, 2048), PSB[:, 0:2048]).then_inc(sDV, 1)
                C.inc("DV", 1)
                @block.sync
                def _(s):
                    s.wait_ge(sDV, C.get("DV"))
                    s.wait_ge(sGP, C.get("GP"))
                    s.dma_start(out=dbgE2.ap()[:, 0:2048], in_=ctx_acc[:, 0:2048]).then_inc(sST, 16)
                    s.dma_start(out=dbgE2.ap()[0:16, 2048:4096], in_=norm_dram.ap()[:, :]).then_inc(sST, 16)
                    s.dma_start(out=dbgE2.ap()[:, 4096:4608].bitcast(BF16)[:, 0:1024],
                                in_=KT_win[:, 512:1536]).then_inc(sST, 16)
                    s.dma_start(out=dbgE2.ap()[:, 4608:5120].bitcast(BF16)[:, 0:1024],
                                in_=A_buf[0][:, 0:1024]).then_inc(sST, 16)
                    s.dma_start(out=dbgE2.ap()[:, 5120:7168], in_=V(A2, 65536 - 16384, F32, 2048)).then_inc(sST, 16)
                    s.dma_start(out=dbgE2.ap()[:, 7168:9216], in_=V(A2, 65536 - 8192, F32, 2048)).then_inc(sST, 16)
                    s.dma_start(out=dbgE2.ap()[:, 9216:10240].bitcast(BF16)[:, 0:2048],
                                in_=A_buf[1]).then_inc(sST, 16)
                C.inc("ST", 16 * 7)
                dbgE3 = nc.dram_tensor("dbgE3", [128, 16512], F32, kind="ExternalOutput")
                @block.sync
                def _(s):
                    s.dma_start(out=dbgE3.ap()[:, 0:8320].bitcast(BF16)[:, 0:16640],
                                in_=V_win[:, :]).then_inc(sST, 16)
                    s.dma_start(out=dbgE3.ap()[:, 8320:16512].bitcast(BF16)[:, 0:16384],
                                in_=qTsb[:, :]).then_inc(sST, 16)
                C.inc("ST", 32)

            rn_cur = V(A4, 0, F32, 2048)
            rn_rec = V(A4, 8192, F32, 2048)
            ctx_tok = V(A1, 0, F32, NT * D)
            tmpn = V(A6, 8192, F32, 512)
            for h in range(H):
                hp, rs = h // 2, (h % 2) * 64
                @block.sync
                def _(s, h=h):
                    s.wait_ge(sDV, C.get("DV"))
                    s.wait_ge(sGP, C.get("GP"))
                    s.wait_ge(sPE, C.get("PE"))
                    s.dma_start(out=rn_cur[0:1, :], in_=norm_dram.ap()[h:h + 1, :]).then_inc(sLD, 16)
                C.inc("LD", 16)
                @block.vector
                def _(v):
                    v.wait_ge(sLD, C.get("LD"))
                    v.reciprocal(rn_rec[0:1, :], rn_cur[0:1, :])
                    v.drain()
                    v.engine_nop().then_inc(sDV, 1)
                C.inc("DV", 1)
                for qb in range(4):
                    @block.tensor
                    def _(t, h=h, qb=qb):
                        t.wait_ge(sDV, C.get("DV"))
                        t.matmul(PSA[0:64, 0:512], ones128_t[0:1, 0:64],
                                 rn_rec[0:1, qb * 512:(qb + 1) * 512],
                                 start=True, stop=True).then_inc(sPE, 1)
                    C.inc("PE", 1)
                    @block.vector
                    def _(v, hp=hp, rs=rs, qb=qb):
                        v.wait_ge(sPE, C.get("PE"))
                        v.tensor_tensor(out=tmpn[0:64, :],
                                        in0=ctx_acc[rs:rs + 64, hp * TL + qb * 512: hp * TL + qb * 512 + 512],
                                        in1=PSA[0:64, 0:512], op=ALU.mult).then_inc(sDV, 1)
                    C.inc("DV", 1)
                    @block.tensor
                    def _(t):
                        t.wait_ge(sDV, C.get("DV"))
                        for sub in range(4):
                            mm = t.transpose(PSA[:, 1024 + sub * 64:1024 + (sub + 1) * 64],
                                             tmpn[0:64, sub * 128:(sub + 1) * 128], ident32_t[0:64, 0:64])
                        mm.then_inc(sPE, 1)
                    C.inc("PE", 1)
                    @block.vector
                    def _(v, h=h, qb=qb):
                        v.wait_ge(sPE, C.get("PE"))
                        ct3 = ctx_tok.rearrange("p (t d) -> p t d", t=NT)
                        dst = ct3[:, qb * 4:(qb + 1) * 4, h * 64:(h + 1) * 64]
                        v.tensor_copy(dst, PSA[:, 1024:1280].rearrange("p (s d) -> p s d", s=4)).then_inc(sDV, 1)
                    C.inc("DV", 1)
            @block.sync
            def _(s):
                s.wait_ge(sDV, C.get("DV"))
                s.dma_start(out=V(A2, 0, F32, NT * D), in_=spill_x.ap()).then_inc(sLD, 16)
                s.dma_start(out=V(A5, 0, BF16, DC * D), in_=tern_sp["o"].ap()).then_inc(sLD, 16)
            C.inc("LD", 32)
            ld_back = C.get("LD")
            xres2 = V(A2, 0, F32, NT * D)
            woT = V(A5, 0, BF16, DC * D)
            ctxqT = V(A6, 6144, BF16, DC * 512)
            for g in range(NG):
                for tg in range(4):
                    i = g * 4 + tg
                    ct = ctx_tok[:, i * D:(i + 1) * D]
                    @block.vector
                    def _(v, ct=ct):
                        v.wait_ge(sPE, C.get("PE"))
                        v.tensor_tensor(out=V(A6, 2048, F32, 512), in0=ct[:, 0:512],
                                        in1=ct[:, 0:512], op=ALU.mult)
                        v.tensor_reduce(ln_sc[:, 11:12], V(A6, 2048, F32, 512),
                                        mybir.AxisListType.X, ALU.add)
                        v.tensor_tensor(out=V(A6, 2048, F32, 512), in0=ct[:, 512:1024],
                                        in1=ct[:, 512:1024], op=ALU.mult)
                        v.tensor_reduce(ln_sc[:, 14:15], V(A6, 2048, F32, 512),
                                        mybir.AxisListType.X, ALU.add)
                        v.drain()
                        v.tensor_tensor(out=ln_sc[:, 11:12], in0=ln_sc[:, 11:12],
                                        in1=ln_sc[:, 14:15], op=ALU.add)
                        v.drain()
                        v.tensor_scalar(out=ln_sc[:, 11:12], in0=ln_sc[:, 11:12],
                                        scalar1=1.0 / D, scalar2=RMS_EPS,
                                        op0=ALU.mult, op1=ALU.add)
                        v.drain()
                        v.reciprocal(ln_sc[:, 12:13], ln_sc[:, 11:12])
                        v.drain()
                        v.engine_nop().then_inc(sDV, 1)
                    C.inc("DV", 1)
                    @block.scalar
                    def _(sc):
                        sc.wait_ge(sDV, C.get("DV"))
                        sc.activation(ln_sc[:, 12:13], ln_sc[:, 12:13], AF.Sqrt).then_inc(sAC, 1)
                    C.inc("AC", 1)
                    @block.vector
                    def _(v, ct=ct):
                        v.wait_ge(sAC, C.get("AC"))
                        emit_rsqrt_refine(v, ln_sc[:, 12:13], ln_sc[:, 11:12],
                                          ln_sc[:, 9:10], ln_sc[:, 10:11])
                        v.reduce_max(ln_sc[:, 5:6], ct, axis=mybir.AxisListType.X,
                                     apply_absolute_value=True)
                        v.drain()
                        v.tensor_tensor(out=ln_sc[:, 5:6], in0=ln_sc[:, 5:6],
                                        in1=ln_sc[:, 12:13], op=ALU.mult)
                        v.drain()
                        v.tensor_scalar(out=ln_sc[:, 6:7], in0=ln_sc[:, 5:6],
                                        scalar1=AQ_EPS, scalar2=1.0 / 127.0,
                                        op0=ALU.max, op1=ALU.mult)
                        v.drain()
                        v.reciprocal(ln_sc[:, 7:8], ln_sc[:, 6:7])
                        v.drain()
                        v.tensor_tensor(out=ln_sc[:, 13:14], in0=ln_sc[:, 12:13],
                                        in1=ln_sc[:, 7:8], op=ALU.mult)
                        v.drain()
                        PSB_i2 = PSB.ap().bitcast(I32)
                        for hh in range(2):
                            v.tensor_scalar(out=PSB_i2[:, 1536:2048], in0=ct[:, hh * 512:(hh + 1) * 512],
                                            scalar1=ln_sc[:, 13:14], scalar2=None, op0=ALU.mult)
                            v.drain()
                            v.tensor_scalar(out=ub_bf[:, hh * 512:(hh + 1) * 512], in0=PSB_i2[:, 1536:2048],
                                            scalar1=ln_sc[:, 6:7], scalar2=None, op0=ALU.mult)
                            v.drain()
                        v.engine_nop().then_inc(sDV, 1)
                    C.inc("DV", 1)
                    @block.tensor
                    def _(t):
                        t.wait_ge(sDV, C.get("DV"))
                        for j in range(DC):
                            mm = t.transpose(PSB_bf[:, 2048 + j * 128:2048 + (j + 1) * 128],
                                             ub_bf[:, j * 128:(j + 1) * 128], identb_t[:, :])
                        mm.then_inc(sPE, 1)
                    C.inc("PE", 1)
                    @block.vector
                    def _(v, tg=tg):
                        v.wait_ge(sPE, C.get("PE"))
                        dst = ctxqT.rearrange("p (j t) -> p j t", j=DC)[:, :, tg * 128:(tg + 1) * 128]
                        v.tensor_copy(dst, PSB_bf[:, 2048:3072].rearrange("p (j t) -> p j t", j=DC)).then_inc(sDV, 1)
                    C.inc("DV", 1)
                for oc in range(DC):
                    @block.tensor
                    def _(t, oc=oc):
                        t.wait_ge(sDV, C.get("DV"))
                        t.wait_ge(sLD, ld_back)
                        for j in range(DC):
                            mm = t.matmul(PSA[:, 0:512],
                                          woT[:, j * D + oc * 128:j * D + (oc + 1) * 128],
                                          ctxqT[:, j * 512:(j + 1) * 512],
                                          start=(j == 0), stop=(j == DC - 1))
                        mm.then_inc(sPE, 1)
                    C.inc("PE", 1)
                    @block.vector
                    def _(v, oc=oc):
                        v.wait_ge(sPE, C.get("PE"))
                        v.tensor_scalar(out=V(A6, 0, F32, 512), in0=PSA[:, 0:512],
                                        scalar1=swinv[:, 3:4], scalar2=None,
                                        op0=ALU.mult).then_inc(sDV, 1)
                    C.inc("DV", 1)
                    @block.tensor
                    def _(t):
                        t.wait_ge(sDV, C.get("DV"))
                        for sub in range(4):
                            mm = t.transpose(PSB[:, 512 + sub * 128:512 + (sub + 1) * 128],
                                             V(A6, 0, F32, 512)[:, sub * 128:(sub + 1) * 128],
                                             ident32_t[:, :])
                        mm.then_inc(sPE, 1)
                    C.inc("PE", 1)
                    @block.vector
                    def _(v, g=g, oc=oc):
                        v.wait_ge(sPE, C.get("PE"))
                        xr3 = xres2.rearrange("p (t d) -> p t d", t=NT)
                        dst = xr3[:, g * 4:(g + 1) * 4, oc * 128:(oc + 1) * 128]
                        v.tensor_tensor(out=dst, in0=dst,
                                        in1=PSB[:, 512:1024].rearrange("p (s d) -> p s d", s=4),
                                        op=ALU.add).then_inc(sDV, 1)
                    C.inc("DV", 1)
            @block.vector
            def _(v):
                v.wait_ge(sDV, C.get("DV"))
                v.tensor_copy(xres, xres2).then_inc(sDV, 1)
            C.inc("DV", 1)


        def l1attn_phase_v2():
            WT = {nm: V(A2, mi * 16384, BF16, DC * D)
                  for mi, nm in enumerate(("q", "k", "v", "o"))}
            agx_in = nc.dram_tensor("agx_in", [TL, D], BF16)
            agx_out = nc.dram_tensor("agx_out", [T, D], BF16)
            sel_in = nc.dram_tensor("sel", [32], F32, kind="ExternalInput")
            for mi, nm in enumerate(("q", "k", "v", "o")):
                @block.sync
                def _(s, mi=mi, nm=nm):
                    s.wait_ge(sDV, C.get("DV"))
                    s.wait_ge(sPE, C.get("PE"))
                    s.dma_start(out=V(A2, mi * 16384, BF16, DC * D),
                                in_=tern_sp[nm].ap()).then_inc(sLD, 16)
            C.inc("LD", 16 * 4)
            sel_row = V(A4, 11008, F32, 32)
            selb = V(A6, 14336, F32, 32)
            @block.sync
            def _(s):
                s.dma_start(out=sel_row[0:1, :], in_=sel_in.ap().unsqueeze(0)).then_inc(sLD, 16)
            C.inc("LD", 16)
            ld_w = C.get("LD")
            @block.tensor
            def _(t):
                t.wait_ge(sLD, ld_w)
                t.wait_ge(sDV, C.get("DV"))
                t.matmul(PSB[:, 0:32], ones128_t[0:1, :], sel_row[0:1, :],
                         start=True, stop=True).then_inc(sPE, 1)
            C.inc("PE", 1)
            @block.vector
            def _(v):
                v.wait_ge(sPE, C.get("PE"))
                v.tensor_copy(selb, PSB[:, 0:32])
                v.drain()
                v.engine_nop().then_inc(sDV, 1)
            C.inc("DV", 1)
            q_sb = V(A4, 0, BF16, D)
            k_sb = V(A4, 2048, BF16, D)
            v_sb = V(A4, 4096, BF16, D)
            ctxb = V(A4, 6144, BF16, D)
            scores = V(A4, 8192, F32, 256)
            e_sb = V(A4, 9216, F32, 256)
            pbuf = [V(A4, 10240, F32, 64), V(A4, 10752, F32, 64)]
            smx = V(A4, 10496, F32, 16)
            negm = V(A4, 10560, F32, 16)
            den = V(A4, 10624, F32, 16)
            rden = V(A4, 10688, F32, 16)
            ctxf = V(A4, 12288, F32, D)

            def emit_rms_actquant():
                @block.vector
                def _(v):
                    v.wait_ge(sPE, C.get("PE"))
                    PSB_f = PSB.ap()
                    v.tensor_tensor(out=PSB_f[:, 1024:1536], in0=u_f32[:, 0:512],
                                    in1=u_f32[:, 0:512], op=ALU.mult)
                    v.tensor_reduce(ln_sc[:, 11:12], PSB_f[:, 1024:1536],
                                    mybir.AxisListType.X, ALU.add)
                    v.tensor_tensor(out=PSB_f[:, 1024:1536], in0=u_f32[:, 512:1024],
                                    in1=u_f32[:, 512:1024], op=ALU.mult)
                    v.tensor_reduce(ln_sc[:, 14:15], PSB_f[:, 1024:1536],
                                    mybir.AxisListType.X, ALU.add)
                    v.drain()
                    v.tensor_tensor(out=ln_sc[:, 11:12], in0=ln_sc[:, 11:12],
                                    in1=ln_sc[:, 14:15], op=ALU.add)
                    v.drain()
                    v.tensor_scalar(out=ln_sc[:, 11:12], in0=ln_sc[:, 11:12],
                                    scalar1=1.0 / D, scalar2=RMS_EPS,
                                    op0=ALU.mult, op1=ALU.add)
                    v.drain()
                    v.reciprocal(ln_sc[:, 12:13], ln_sc[:, 11:12])
                    v.drain()
                    v.engine_nop().then_inc(sDV, 1)
                C.inc("DV", 1)
                @block.scalar
                def _(sc):
                    sc.wait_ge(sDV, C.get("DV"))
                    sc.activation(ln_sc[:, 12:13], ln_sc[:, 12:13], AF.Sqrt).then_inc(sAC, 1)
                C.inc("AC", 1)
                @block.vector
                def _(v):
                    v.wait_ge(sAC, C.get("AC"))
                    emit_rsqrt_refine(v, ln_sc[:, 12:13], ln_sc[:, 11:12],
                                      ln_sc[:, 9:10], ln_sc[:, 10:11])
                    v.reduce_max(ln_sc[:, 5:6], u_f32, axis=mybir.AxisListType.X,
                                 apply_absolute_value=True)
                    v.drain()
                    v.tensor_tensor(out=ln_sc[:, 5:6], in0=ln_sc[:, 5:6],
                                    in1=ln_sc[:, 12:13], op=ALU.mult)
                    v.drain()
                    v.tensor_scalar(out=ln_sc[:, 6:7], in0=ln_sc[:, 5:6],
                                    scalar1=AQ_EPS, scalar2=1.0 / 127.0,
                                    op0=ALU.max, op1=ALU.mult)
                    v.drain()
                    v.reciprocal(ln_sc[:, 7:8], ln_sc[:, 6:7])
                    v.drain()
                    v.tensor_tensor(out=ln_sc[:, 13:14], in0=ln_sc[:, 12:13],
                                    in1=ln_sc[:, 7:8], op=ALU.mult)
                    v.drain()
                    PSB_i = PSB.ap().bitcast(I32)
                    for hh in range(2):
                        v.tensor_scalar(out=PSB_i[:, 1536:2048], in0=u_f32[:, hh * 512:(hh + 1) * 512],
                                        scalar1=ln_sc[:, 13:14], scalar2=None, op0=ALU.mult)
                        v.drain()
                        v.tensor_scalar(out=ub_bf[:, hh * 512:(hh + 1) * 512], in0=PSB_i[:, 1536:2048],
                                        scalar1=ln_sc[:, 6:7], scalar2=None, op0=ALU.mult)
                        v.drain()
                    v.engine_nop().then_inc(sDV, 1)
                C.inc("DV", 1)

            for g in range(NG):
                for tg in range(4):
                    i = g * 4 + tg
                    emit_tile_ln(xres[:, i * D:(i + 1) * D])
                    emit_rms_actquant()
                    emit_transpose_to_stage(ub_bf, tg)
                for tg in range(4):
                    i = g * 4 + tg
                    @block.tensor
                    def _(t, tg=tg):
                        t.wait_ge(sDV, C.get("DV"))
                        for nm, off in (("q", 0), ("k", 1024)):
                            for j in range(DC):
                                lhsT = uT_stage[:, j * 512 + tg * 128: j * 512 + (tg + 1) * 128]
                                t.matmul(PSA[:, off:off + 512], lhsT,
                                         WT[nm][:, j * D:j * D + 512],
                                         start=(j == 0), stop=(j == DC - 1))
                                t.matmul(PSA[:, off + 512:off + 1024], lhsT,
                                         WT[nm][:, j * D + 512:(j + 1) * D],
                                         start=(j == 0), stop=(j == DC - 1))
                        for j in range(DC):
                            lhsT = uT_stage[:, j * 512 + tg * 128: j * 512 + (tg + 1) * 128]
                            t.matmul(PSB[:, 0:512], lhsT, WT["v"][:, j * D:j * D + 512],
                                     start=(j == 0), stop=(j == DC - 1))
                            mm = t.matmul(PSB[:, 512:1024], lhsT, WT["v"][:, j * D + 512:(j + 1) * D],
                                          start=(j == 0), stop=(j == DC - 1))
                        mm.then_inc(sPE, 1)
                    C.inc("PE", 1)
                    @block.vector
                    def _(v):
                        v.wait_ge(sPE, C.get("PE"))
                        v.wait_ge(sGP, C.get("GP"))
                        v.tensor_scalar(out=q_sb, in0=PSA[:, 0:1024], scalar1=swinv[:, 0:1],
                                        scalar2=None, op0=ALU.mult)
                        v.tensor_scalar(out=k_sb, in0=PSA[:, 1024:2048], scalar1=swinv[:, 1:2],
                                        scalar2=None, op0=ALU.mult)
                        v.tensor_scalar(out=v_sb, in0=PSB[:, 0:1024], scalar1=swinv[:, 2:3],
                                        scalar2=None, op0=ALU.mult)
                        v.drain()
                        pairs = [(h, tt) for h in range(H) for tt in range(H)]
                        for n, (h, tt) in enumerate(pairs):
                            v.tensor_tensor(out=pbuf[n % 2], in0=q_sb[:, h * 64:(h + 1) * 64],
                                            in1=k_sb[:, tt * 64:(tt + 1) * 64], op=ALU.mult)
                            if n > 0:
                                hp_, tp_ = pairs[n - 1]
                                v.tensor_reduce(scores[:, hp_ * H + tp_: hp_ * H + tp_ + 1],
                                                pbuf[(n - 1) % 2], mybir.AxisListType.X, ALU.add)
                        v.drain()
                        v.tensor_reduce(scores[:, 255:256], pbuf[255 % 2],
                                        mybir.AxisListType.X, ALU.add)
                        v.drain()
                        for h in range(H):
                            v.reduce_max(smx[:, h:h + 1], scores[:, h * H:(h + 1) * H],
                                         axis=mybir.AxisListType.X)
                        v.drain()
                        v.tensor_scalar(out=negm, in0=smx, scalar1=-0.125, scalar2=None,
                                        op0=ALU.mult)
                        v.drain()
                        v.engine_nop().then_inc(sDV, 1)
                    C.inc("DV", 1)
                    @block.scalar
                    def _(sc):
                        sc.wait_ge(sDV, C.get("DV"))
                        for h in range(H):
                            a = sc.activation(e_sb[:, h * H:(h + 1) * H], scores[:, h * H:(h + 1) * H],
                                              AF.Exp, bias=negm[:, h:h + 1], scale=0.125)
                        a.then_inc(sAC, 1)
                    C.inc("AC", 1)
                    @block.vector
                    def _(v):
                        v.wait_ge(sAC, C.get("AC"))
                        for h in range(H):
                            v.tensor_reduce(den[:, h:h + 1], e_sb[:, h * H:(h + 1) * H],
                                            mybir.AxisListType.X, ALU.add)
                        v.drain()
                        v.reciprocal(rden, den)
                        v.drain()
                        for h in range(H):
                            v.tensor_scalar(out=ctxf[:, h * 64:(h + 1) * 64], in0=v_sb[:, 0:64],
                                            scalar1=e_sb[:, h * H:h * H + 1], scalar2=None,
                                            op0=ALU.mult)
                            for tt in range(1, H):
                                v.scalar_tensor_tensor(out=ctxf[:, h * 64:(h + 1) * 64],
                                                       in0=v_sb[:, tt * 64:(tt + 1) * 64],
                                                       scalar=e_sb[:, h * H + tt:h * H + tt + 1],
                                                       in1=ctxf[:, h * 64:(h + 1) * 64],
                                                       op0=ALU.mult, op1=ALU.add)
                            v.tensor_scalar(out=ctxf[:, h * 64:(h + 1) * 64],
                                            in0=ctxf[:, h * 64:(h + 1) * 64],
                                            scalar1=rden[:, h:h + 1], scalar2=None, op0=ALU.mult)
                        v.drain()
                        v.tensor_copy(ctxb, ctxf)
                        v.drain()
                        v.engine_nop().then_inc(sDV, 1)
                    C.inc("DV", 1)
                    @block.gpsimd
                    def _(g_, i=i):
                        g_.wait_ge(sDV, C.get("DV"))
                        ag3 = agx_in.ap().rearrange("(hh sq) c -> hh sq c", hh=16)
                        for h in range(H):
                            g_.dma_start(
                                out=ag3[h:h + 1, i * 8:(i + 1) * 8, :]
                                .rearrange("one sq (r d) -> (one sq r) d", r=16),
                                in_=ctxb[:, h * 64:(h + 1) * 64]).then_inc(sGP, 16)
                    C.inc("GP", 16 * 16)
            @block.gpsimd
            def _(g_):
                g_.wait_ge(sGP, C.get("GP"))
                g_.collective_compute(
                    "AllGather", ALU.bypass, replica_groups=[list(range(NCORE))],
                    ins=[agx_in.ap().opt()], outs=[agx_out.ap().opt()]).then_inc(sCC, 1)
            C.inc("CC", 1)
            cc2 = C.get("CC")
            blockbuf = V(A5, 0, BF16, 16 * D)
            ctx2 = V(A4, 0, BF16, NT * D)
            pickf = V(A6, 0, F32, D)
            for cc in range(NCORE):
                @block.sync
                def _(s, cc=cc):
                    if cc == 0:
                        s.wait_ge(sCC, cc2)
                        s.wait_ge(sDV, C.get("DV"))
                    else:
                        s.wait_ge(sDV, C.get("DV"))
                    s.dma_start(out=blockbuf.rearrange("p (m c) -> p m c", m=16),
                                in_=agx_out.ap()[cc * TL:(cc + 1) * TL, :]
                                .rearrange("(m p) c -> p m c", p=128)).then_inc(sLD, 16)
                C.inc("LD", 16)
                @block.vector
                def _(v, cc=cc):
                    v.wait_ge(sLD, C.get("LD"))
                    for u in range(2):
                        tdst = u * 8 + cc
                        v.tensor_scalar(out=pickf, in0=blockbuf[:, 0:1024],
                                        scalar1=selb[:, u * 16:u * 16 + 1], scalar2=None,
                                        op0=ALU.mult)
                        for m in range(1, 16):
                            v.scalar_tensor_tensor(out=pickf, in0=blockbuf[:, m * 1024:(m + 1) * 1024],
                                                   scalar=selb[:, u * 16 + m:u * 16 + m + 1],
                                                   in1=pickf, op0=ALU.mult, op1=ALU.add)
                        v.drain()
                        v.tensor_copy(ctx2[:, tdst * 1024:(tdst + 1) * 1024], pickf)
                        v.drain()
                    v.engine_nop().then_inc(sDV, 1)
                C.inc("DV", 1)
            for g in range(NG):
                for tg in range(4):
                    i = g * 4 + tg
                    @block.vector
                    def _(v, i=i):
                        v.wait_ge(sPE, C.get("PE"))
                        v.tensor_copy(u_f32, ctx2[:, i * D:(i + 1) * D])
                        v.drain()
                        v.engine_nop().then_inc(sDV, 1)
                    C.inc("DV", 1)
                    emit_rms_actquant()
                    emit_transpose_to_stage(ub_bf, tg)
                for tg in range(4):
                    i = g * 4 + tg
                    @block.tensor
                    def _(t, tg=tg):
                        t.wait_ge(sDV, C.get("DV"))
                        for j in range(DC):
                            lhsT = uT_stage[:, j * 512 + tg * 128: j * 512 + (tg + 1) * 128]
                            t.matmul(PSB[:, 0:512], lhsT, WT["o"][:, j * D:j * D + 512],
                                     start=(j == 0), stop=(j == DC - 1))
                            mm = t.matmul(PSB[:, 512:1024], lhsT, WT["o"][:, j * D + 512:(j + 1) * D],
                                          start=(j == 0), stop=(j == DC - 1))
                        mm.then_inc(sPE, 1)
                    C.inc("PE", 1)
                    @block.vector
                    def _(v, i=i):
                        v.wait_ge(sPE, C.get("PE"))
                        v.tensor_scalar(out=pickf, in0=PSB[:, 0:1024], scalar1=swinv[:, 3:4],
                                        scalar2=None, op0=ALU.mult)
                        v.drain()
                        xt = xres[:, i * D:(i + 1) * D]
                        v.tensor_tensor(out=xt, in0=xt, in1=pickf, op=ALU.add)
                        v.drain()
                        v.engine_nop().then_inc(sDV, 1)
                    C.inc("DV", 1)

        @block.tensor
        def _(t):
            t.wait_ge(sDV, w8t_dv)
            t.wait_ge(sLD, n_init_ld)
        @block.vector
        def _(v):
            v.wait_ge(sDV, w8t_dv)
            v.wait_ge(sLD, n_init_ld)

        tern_prep()
        l0attn_wprep()

        if phases >= 1:
            l0ffn_phase("A")
        if phases >= 2:
            l0attn_phase()
        if phases >= 3:
            l0ffn_phase("C")
        if phases >= 4:
            l1ffn_phase()
        if phases >= 5:
            l1attn_phase_v2()
        if phases >= 6:
            l1ffn_phase()

        @block.sync
        def _(s):
            s.wait_ge(sDV, C.get("DV"))
            s.wait_ge(sPE, C.get("PE"))
            s.wait_ge(sAC, C.get("AC"))
            s.dma_start(out=y_out.ap().rearrange("(t p) d -> p t d", p=128),
                        in_=xres.rearrange("p (t d) -> p t d", t=NT)).then_inc(sST, 16)
            s.wait_ge(sST, C.inc("ST", 16))

    return nc


def kernel(L0_attn_query, L0_attn_Wq, L0_attn_bq, L0_attn_Wk, L0_attn_bk,
           L0_attn_Wv, L0_attn_bv, L0_attn_Wo, L0_attn_bo,
           L0_ffn_weight_q, L0_ffn_weight_scale, L0_ffn_bias,
           L1_attn_wq, L1_attn_wk, L1_attn_wv, L1_attn_wo,
           L1_ffn_w1, L1_ffn_b1, L1_ffn_w2, L1_ffn_b2):
    x = np.ascontiguousarray(np.asarray(L0_attn_query, dtype=np.float32).reshape(T, D))
    shards = [np.ascontiguousarray(x[c * TL:(c + 1) * TL]) for c in range(NCORE)]
    common = {
        "l0w8": np.ascontiguousarray(np.asarray(L0_ffn_weight_q, dtype=np.int8)),
        "l0ws": np.ascontiguousarray(np.asarray(L0_ffn_weight_scale, dtype=np.float32)),
        "l0fb": np.ascontiguousarray(np.asarray(L0_ffn_bias, dtype=np.float32)),
        "l0a_wq": np.ascontiguousarray(np.asarray(L0_attn_Wq, dtype=np.float32)),
        "l0a_bq": np.ascontiguousarray(np.asarray(L0_attn_bq, dtype=np.float32)),
        "l0a_wk": np.ascontiguousarray(np.asarray(L0_attn_Wk, dtype=np.float32)),
        "l0a_bk": np.ascontiguousarray(np.asarray(L0_attn_bk, dtype=np.float32)),
        "l0a_wv": np.ascontiguousarray(np.asarray(L0_attn_Wv, dtype=np.float32)),
        "l0a_bv": np.ascontiguousarray(np.asarray(L0_attn_bv, dtype=np.float32)),
        "l0a_wo": np.ascontiguousarray(np.asarray(L0_attn_Wo, dtype=np.float32)),
        "l0a_bo": np.ascontiguousarray(np.asarray(L0_attn_bo, dtype=np.float32)),
        "l1a_wq": np.ascontiguousarray(np.asarray(L1_attn_wq, dtype=np.float32)),
        "l1a_wk": np.ascontiguousarray(np.asarray(L1_attn_wk, dtype=np.float32)),
        "l1a_wv": np.ascontiguousarray(np.asarray(L1_attn_wv, dtype=np.float32)),
        "l1a_wo": np.ascontiguousarray(np.asarray(L1_attn_wo, dtype=np.float32)),
        "w1": np.ascontiguousarray(np.asarray(L1_ffn_w1, dtype=np.float32)),
        "b1": np.ascontiguousarray(np.asarray(L1_ffn_b1, dtype=np.float32)),
        "w2": np.ascontiguousarray(np.asarray(L1_ffn_w2, dtype=np.float32)),
        "b2": np.ascontiguousarray(np.asarray(L1_ffn_b2, dtype=np.float32)),
    }
    nc = _build(PHASES)
    sels = []
    for c in range(NCORE):
        sel = np.zeros(32, dtype=np.float32)
        for u in range(2):
            sel[u * 16 + 2 * c + u] = 1.0
        sels.append(sel)
    in_maps = [dict(common, x=shards[c], **({"sel": sels[c]} if PHASES >= 5 else {}))
               for c in range(NCORE)]
    res = run_bass_kernel_spmd(nc, in_maps, core_ids=list(range(NCORE)))
    out = np.concatenate([np.asarray(r["y"], dtype=np.float32) for r in res.results], axis=0)
    return out.reshape(1, T, D)
